# version 02
# baseline (speedup 1.0000x reference)
import numpy as np
import ml_dtypes
import concourse.bass as bass
import concourse.mybir as mybir
from concourse.bass_utils import run_bass_kernel_spmd

F32 = mybir.dt.float32
BF16 = mybir.dt.bfloat16
AF = mybir.ActivationFunctionType
ALU = mybir.AluOpType
AX = mybir.AxisListType

RNE_C = 12582912.0
LAMBDA_INIT = 0.8 - 0.6 * float(np.exp(-0.3))

T = 2048
D = 2048
NTT = T // 128
NDC = D // 128

bf16 = ml_dtypes.bfloat16


class S:
    def __init__(self, nc, name):
        self.h = nc.alloc_semaphore(name)
        self.n = 0

    def inc(self, ins, k=1):
        ins.then_inc(self.h, k)
        self.n += k
        return self.n

    def wait(self, eng, val=None):
        eng.wait_ge(self.h, self.n if val is None else val)


class Chain:
    def __init__(self, sem):
        self.sem = sem

    def step(self, eng, ins):
        self.sem.inc(ins, 1)
        eng.wait_ge(self.sem.h, self.sem.n)
        return ins


class Ctx:
    def __init__(self, nc, blk):
        self.nc = nc
        self.blk = blk
        self.sems = {}
        self.sb = {}
        self.ps = {}
        self.dram = {}

    def sem(self, name):
        if name not in self.sems:
            self.sems[name] = S(self.nc, name)
        return self.sems[name]

    def chain(self, name):
        return Chain(self.sem("chain_" + name))


def alloc_common(ctx):
    nc = ctx.nc
    ctx.dram["ident_bf"] = nc.inline_tensor(np.eye(128, dtype=bf16), name="c_ident_bf").ap()
    ctx.dram["ident_f"] = nc.inline_tensor(np.eye(128, dtype=np.float32), name="c_ident_f").ap()
    ctx.dram["ones_col_f"] = nc.inline_tensor(np.ones((128, 1), dtype=np.float32), name="c_ones_f").ap()
    ctx.dram["ones_row_bf"] = nc.inline_tensor(np.ones((1, 128), dtype=bf16), name="c_ones_row").ap()
    ctx.dram["ones_row_f"] = nc.inline_tensor(np.ones((1, 128), dtype=np.float32), name="c_ones_row_f").ap()
    misc = np.zeros((1, 16), dtype=np.float32)
    misc[0, 0] = float(np.finfo(np.float32).eps)
    misc[0, 1] = 1.0 / 8192
    misc[0, 2] = 1e-6
    misc[0, 3] = 1e-5
    ctx.dram["misc"] = nc.inline_tensor(misc, name="c_misc").ap()
    ctx.dram["misc_col"] = nc.inline_tensor(np.tile(misc, (128, 1)), name="c_misc_col").ap()
    ctx.sb["ident_bf"] = nc.alloc_sbuf_tensor("ident_bf", [128, 128], BF16)
    ctx.sb["ident_f"] = nc.alloc_sbuf_tensor("ident_f", [128, 128], F32)
    ctx.sb["ones_col_f"] = nc.alloc_sbuf_tensor("ones_col_f", [128, 1], F32)
    ctx.sb["ones_col_bf"] = nc.alloc_sbuf_tensor("ones_col_bf", [128, 1], BF16)
    ctx.sb["ones_row_bf"] = nc.alloc_sbuf_tensor("ones_row_bf", [1, 128], BF16)
    ctx.sb["ones_row_f"] = nc.alloc_sbuf_tensor("ones_row_f", [1, 128], F32)
    ctx.sb["misc"] = nc.alloc_sbuf_tensor("misc", [1, 16], F32)
    ctx.sb["misc_col"] = nc.alloc_sbuf_tensor("misc_col", [128, 16], F32)
    ctx.ps["mm"] = [nc.alloc_psum_tensor(f"ps_mm{i}", [128, 512], F32) for i in range(4)]
    ctx.ps["trb"] = [nc.alloc_psum_tensor(f"ps_trb{i}", [128, 512], BF16) for i in range(2)]
    ctx.ps["tr32"] = nc.alloc_psum_tensor("ps_tr32", [128, 128], F32)
    ctx.ps["row"] = nc.alloc_psum_tensor("ps_row", [1, 512], F32)

    ld = ctx.sem("const_ld")

    @ctx.blk.sync
    def _(sync):
        ld.inc(sync.dma_start(out=ctx.sb["ident_bf"][:], in_=ctx.dram["ident_bf"]), 16)
        ld.inc(sync.dma_start(out=ctx.sb["ident_f"][:], in_=ctx.dram["ident_f"]), 16)
        ld.inc(sync.dma_start(out=ctx.sb["ones_col_f"][:], in_=ctx.dram["ones_col_f"]), 16)
        ld.inc(sync.dma_start(out=ctx.sb["ones_row_bf"][:], in_=ctx.dram["ones_row_bf"]), 16)
        ld.inc(sync.dma_start(out=ctx.sb["ones_row_f"][:], in_=ctx.dram["ones_row_f"]), 16)
        ld.inc(sync.dma_start(out=ctx.sb["misc"][:], in_=ctx.dram["misc"]), 16)
        ld.inc(sync.dma_start(out=ctx.sb["misc_col"][:], in_=ctx.dram["misc_col"]), 16)

    @ctx.blk.vector
    def _(v):
        ld.wait(v)
        ctx.sem("const_rdy").inc(v.tensor_copy(ctx.sb["ones_col_bf"][:], ctx.sb["ones_col_f"][:]), 1)


def emit_ln(ctx, tag, x_dram, out_dram, m0_dram=None, n_tiles=NTT, eps=1e-5,
            out32_dram=None):
    nc = ctx.nc
    NG = NDC // 4
    sload = [ctx.sem(tag + f"_xld{j}") for j in range(2)]
    smld = [ctx.sem(tag + f"_mld{j}") for j in range(2)]
    sst = [ctx.sem(tag + f"_st{j}") for j in range(2)]
    sst32 = [ctx.sem(tag + f"_st32{j}") for j in range(2)] if out32_dram is not None else None
    xn32 = nc.alloc_sbuf_tensor(tag + "_xn32", [128, 2 * D], F32) if out32_dram is not None else None
    sintr = ctx.sem(tag + "_intr")
    sxr = ctx.sem(tag + "_xr")
    sstep = ctx.sem(tag + "_step")
    sapp = ctx.sem(tag + "_app")
    souttr = ctx.sem(tag + "_outtr")
    sevo = ctx.sem(tag + "_evo")

    xa = nc.alloc_sbuf_tensor(tag + "_xa", [128, 2 * D], F32)
    ma = nc.alloc_sbuf_tensor(tag + "_ma", [128, 2 * 128 * NDC], BF16) if m0_dram is not None else None
    xr = nc.alloc_sbuf_tensor(tag + "_xr", [128, D], F32)
    sq = nc.alloc_sbuf_tensor(tag + "_sq", [128, D], F32)
    stats = nc.alloc_sbuf_tensor(tag + "_stats", [128, 8], F32)
    xnb = nc.alloc_sbuf_tensor(tag + "_xnb", [128, D], BF16)
    outb = nc.alloc_sbuf_tensor(tag + "_outb", [128, 2 * D], BF16)

    vch = ctx.chain(tag + "_v")
    sch = ctx.chain(tag + "_s")
    crdy = ctx.sem("const_rdy")
    trb = ctx.ps["trb"]

    def issue_load(sync, i):
        j = i % 2
        sload[j].inc(sync.dma_start(out=xa[:, j * D:j * D + D],
                                    in_=x_dram[i * 128:(i + 1) * 128, :]), 16)
        if m0_dram is not None:
            msrc = m0_dram.rearrange("(c p) t -> p c t", p=128)[:, :, i * 128:(i + 1) * 128]
            mdst = ma[:, j * (128 * NDC):(j + 1) * (128 * NDC)].rearrange("p (c f) -> p c f", f=128)
            smld[j].inc(sync.dma_start(out=mdst, in_=msrc), 16)

    @ctx.blk.sync
    def _(sync):
        for i in range(min(2, n_tiles)):
            issue_load(sync, i)

    for i in range(n_tiles):
        j = i % 2
        buf = xa[:, j * D:j * D + D]

        if m0_dram is not None:
            mbuf = ma[:, j * (128 * NDC):(j + 1) * (128 * NDC)]

            @ctx.blk.tensor
            def _(t, i=i, j=j, mbuf=mbuf):
                t.wait_ge(smld[j].h, (i // 2 + 1) * 16)
                t.wait_ge(sevo.h, i * NG)
                crdy.wait(t)
                for g in range(NG):
                    kg = i * NG + g + 1
                    if kg > 2:
                        t.wait_ge(sxr.h, kg - 2)
                    for c in range(4 * g, 4 * g + 4):
                        ins = t.transpose(trb[g % 2][:, (c % 4) * 128:(c % 4) * 128 + 128],
                                          mbuf[:, c * 128:(c + 1) * 128], ctx.sb["ident_bf"][:])
                    sintr.inc(ins, 1)

            @ctx.blk.vector
            def _(v, i=i, j=j, buf=buf):
                v.wait_ge(sload[j].h, (i // 2 + 1) * 16)
                v.wait_ge(sapp.h, i)
                for g in range(NG):
                    kg = i * NG + g + 1
                    v.wait_ge(sintr.h, kg)
                    for c in range(4 * g, 4 * g + 4):
                        ins = v.tensor_add(xr[:, c * 128:(c + 1) * 128], buf[:, c * 128:(c + 1) * 128],
                                           trb[g % 2][:, (c % 4) * 128:(c % 4) * 128 + 128])
                    sxr.inc(ins, 1)
        else:
            @ctx.blk.vector
            def _(v, i=i, j=j, buf=buf):
                v.wait_ge(sload[j].h, (i // 2 + 1) * 16)
                v.wait_ge(sapp.h, i)
                sxr.inc(v.tensor_copy(xr[:], buf), NG)

        @ctx.blk.scalar
        def _(s, i=i):
            s.wait_ge(sxr.h, (i + 1) * NG)
            sch.step(s, s.activation(sq[:], xr[:], AF.Square, accum_out=stats[:, 1:2]))
            sstep.inc(s.nop(), 1)

        @ctx.blk.vector
        def _(v, i=i):
            v.wait_ge(sxr.h, (i + 1) * NG)
            vch.step(v, v.reduce_sum(stats[:, 0:1], xr[:], axis=AX.X))
            sstep.inc(v.nop(), 1)
            v.wait_ge(sstep.h, 5 * i + 2)
            vch.step(v, v.tensor_scalar(stats[:, 2:3], stats[:, 0:1], 1.0 / D, None, ALU.mult))
            vch.step(v, v.tensor_scalar(stats[:, 3:4], stats[:, 1:2], 1.0 / D, None, ALU.mult))
            vch.step(v, v.scalar_tensor_tensor(stats[:, 3:4], stats[:, 2:3], stats[:, 2:3],
                                               stats[:, 3:4], ALU.mult, ALU.subtract))
            vch.step(v, v.tensor_scalar(stats[:, 3:4], stats[:, 3:4], -1.0, eps, ALU.mult, ALU.add))
            sstep.inc(v.nop(), 1)

        @ctx.blk.scalar
        def _(s, i=i):
            s.wait_ge(sstep.h, 5 * i + 3)
            sch.step(s, s.activation(stats[:, 4:5], stats[:, 3:4], AF.Ln))
            sch.step(s, s.activation(stats[:, 4:5], stats[:, 4:5], AF.Exp, scale=-0.5))
            sstep.inc(s.nop(), 1)

        @ctx.blk.vector
        def _(v, i=i):
            v.wait_ge(sstep.h, 5 * i + 4)
            vch.step(v, v.scalar_tensor_tensor(stats[:, 5:6], stats[:, 2:3], -1.0,
                                               stats[:, 4:5], ALU.mult, ALU.mult))
            sstep.inc(v.nop(), 1)

        @ctx.blk.scalar
        def _(s, i=i):
            s.wait_ge(sstep.h, 5 * i + 5)
            if i > 0:
                s.wait_ge(souttr.h, i * NG)
            sch.step(s, s.activation(xnb[:], xr[:], AF.Identity, bias=stats[:, 5:6], scale=stats[:, 4:5]))
            if out32_dram is not None:
                j = i % 2
                if i >= 2:
                    s.wait_ge(sst32[j].h, (i // 2) * 16)
                sch.step(s, s.activation(xn32[:, j * D:(j + 1) * D], xr[:], AF.Identity,
                                         bias=stats[:, 5:6], scale=stats[:, 4:5]))
            sapp.inc(s.nop(), 1)

        @ctx.blk.tensor
        def _(t, i=i):
            t.wait_ge(sapp.h, i + 1)
            crdy.wait(t)
            if m0_dram is not None:
                t.wait_ge(sxr.h, (i + 1) * NG)
            for g in range(NG):
                kg = i * NG + g + 1
                if kg > 2:
                    t.wait_ge(sevo.h, kg - 2)
                for c in range(4 * g, 4 * g + 4):
                    ins = t.transpose(trb[g % 2][:, (c % 4) * 128:(c % 4) * 128 + 128],
                                      xnb[:, c * 128:(c + 1) * 128], ctx.sb["ident_bf"][:])
                souttr.inc(ins, 1)

        @ctx.blk.vector
        def _(v, i=i, j=j):
            if i >= 2:
                v.wait_ge(sst[j].h, (i // 2) * 16)
            for g in range(NG):
                kg = i * NG + g + 1
                v.wait_ge(souttr.h, kg)
                for c in range(4 * g, 4 * g + 4):
                    ins = v.tensor_copy(outb[:, j * D + c * 128:j * D + (c + 1) * 128],
                                        trb[g % 2][:, (c % 4) * 128:(c % 4) * 128 + 128])
                sevo.inc(ins, 1)

        @ctx.blk.sync
        def _(sync, i=i, j=j):
            if i + 2 < n_tiles:
                sync.wait_ge(sapp.h, i + 1)
                issue_load(sync, i + 2)
            sync.wait_ge(sevo.h, (i + 1) * NG)
            osrc = outb[:, j * D:j * D + D].rearrange("p (c f) -> p c f", f=128)
            dst = out_dram[:, i * 128:(i + 1) * 128].rearrange("(c p) f -> p c f", p=128)
            sst[j].inc(sync.dma_start(out=dst, in_=osrc), 16)
            if out32_dram is not None:
                sync.wait_ge(sapp.h, i + 1)
                sst32[j].inc(sync.dma_start(out=out32_dram[i * 128:(i + 1) * 128, :],
                                            in_=xn32[:, j * D:(j + 1) * D]), 16)
    if out32_dram is not None:
        return sst + sst32
    return sst


def emit_qkv(ctx, tag, xn_dram, w_dram, q_dram, kv_dram):
    nc = ctx.nc
    W = 3 * D
    NOB = W // 512
    sxld = ctx.sem(tag + "_xld")
    swld = ctx.sem(tag + "_wld")
    sps = ctx.sem(tag + "_ps")
    sev = ctx.sem(tag + "_ev")
    sstq = [ctx.sem(tag + f"_st{j}") for j in range(2)]

    SW = NDC * 512
    xn = nc.alloc_sbuf_tensor(tag + "_xn", [128, NDC * T], BF16)
    wsb = nc.alloc_sbuf_tensor(tag + "_w", [128, 2 * SW], BF16)
    ob_buf = nc.alloc_sbuf_tensor(tag + "_ob", [128, 2 * 512], BF16)
    mm = ctx.ps["mm"]
    swl = [ctx.sem(tag + f"_wl{j}") for j in range(2)]

    @ctx.blk.sync
    def _(sync):
        src = xn_dram.rearrange("(c p) t -> p c t", p=128)
        sxld.inc(sync.dma_start(out=xn[:].rearrange("p (c t) -> p c t", t=T), in_=src), 16)

    @ctx.blk.gpsimd
    def _(g):
        for ob in range(NOB):
            if ob >= 2:
                g.wait_ge(sps.h, (ob - 1) * NTT)
            src = w_dram.rearrange("(k p) o -> p k o", p=128)[:, :, ob * 512:(ob + 1) * 512]
            dst = wsb[:, (ob % 2) * SW:(ob % 2 + 1) * SW].rearrange("p (k o) -> p k o", o=512)
            swl[ob % 2].inc(g.dma_start(out=dst, in_=src), 16)

    ntile = 0
    for ob in range(NOB):
        for tt in range(NTT):
            n = ntile; ntile += 1

            @ctx.blk.tensor
            def _(t, ob=ob, tt=tt, n=n):
                if n == 0:
                    sxld.wait(t)
                if tt == 0:
                    t.wait_ge(swl[ob % 2].h, (ob // 2 + 1) * 16)
                if n >= 4:
                    t.wait_ge(sev.h, n - 3)
                base = (ob % 2) * SW
                for k in range(NDC):
                    ins = t.matmul(mm[n % 4][:],
                                   xn[:, k * T + tt * 128: k * T + tt * 128 + 128],
                                   wsb[:, base + k * 512: base + (k + 1) * 512],
                                   start=(k == 0), stop=(k == NDC - 1))
                sps.inc(ins, 1)

            @ctx.blk.vector
            def _(v, n=n):
                v.wait_ge(sps.h, n + 1)
                if n >= 2:
                    v.wait_ge(sstq[n % 2].h, (n // 2) * 16)
                sev.inc(v.tensor_copy(ob_buf[:, (n % 2) * 512:(n % 2) * 512 + 512], mm[n % 4][:]), 1)

            @ctx.blk.sync
            def _(sync, ob=ob, tt=tt, n=n):
                sync.wait_ge(sev.h, n + 1)
                if ob < 4:
                    dst = q_dram[tt * 128:(tt + 1) * 128, ob * 512:(ob + 1) * 512]
                else:
                    dst = kv_dram[tt * 128:(tt + 1) * 128, (ob - 4) * 512:(ob - 3) * 512]
                sstq[n % 2].inc(sync.dma_start(out=dst, in_=ob_buf[:, (n % 2) * 512:(n % 2) * 512 + 512]), 16)
    return sstq


def build_qkv_test():
    nc = bass.Bass("TRN2", target_bir_lowering=False, debug=False)
    xn_d = nc.dram_tensor("xnT", [D, T], BF16, kind="ExternalInput").ap()
    w_d = nc.dram_tensor("w_qkv", [D, 3 * D], F32, kind="ExternalInput").ap()
    q_d = nc.dram_tensor("q", [T, D], BF16, kind="ExternalOutput").ap()
    kv_d = nc.dram_tensor("kv", [T, 2 * D], BF16, kind="ExternalOutput").ap()
    with nc.Block() as blk:
        ctx = Ctx(nc, blk)
        alloc_common(ctx)
        ss = emit_qkv(ctx, "qkv", xn_d, w_d, q_d, kv_d)

        @blk.sync
        def _(sync):
            for s in ss:
                s.wait(sync)
    return nc


def test_qkv():
    rng = np.random.default_rng(1)
    xnT = rng.standard_normal((D, T)).astype(bf16)
    w = (rng.standard_normal((D, 3 * D)) * 0.02).astype(np.float32)
    nc = build_qkv_test()
    res = run_bass_kernel_spmd(nc, [dict(xnT=xnT, w_qkv=w)], core_ids=[0])
    ref = xnT.astype(np.float32).T @ w.astype(bf16).astype(np.float32)
    got = np.concatenate([res.results[0]["q"], res.results[0]["kv"]], 1).astype(np.float32)
    refb = ref.astype(bf16).astype(np.float32)
    rel = np.sqrt(((got - refb) ** 2).mean() / (refb ** 2).mean())
    print(f"QKV: relL2 {rel:.3e}")
    assert rel < 5e-3


def emit_quant(ctx, tag, w_dram, out_dram, R, C):
    nc = ctx.nc
    NT = R // 128
    G = C // 64
    wb = nc.alloc_sbuf_tensor(tag + "_wb", [128, 2 * C], BF16)
    sc = nc.alloc_sbuf_tensor(tag + "_sc", [128, G], F32)
    scb = nc.alloc_sbuf_tensor(tag + "_scb", [128, G], BF16)
    sce = nc.alloc_sbuf_tensor(tag + "_sce", [128, C], BF16)
    rcp = nc.alloc_sbuf_tensor(tag + "_rcp", [128, G], F32)
    rce = nc.alloc_sbuf_tensor(tag + "_rce", [128, C], F32)
    qd = nc.alloc_sbuf_tensor(tag + "_qd", [128, C], BF16)
    ot = nc.alloc_sbuf_tensor(tag + "_ot", [128, 2 * C], BF16)
    sld = [ctx.sem(tag + f"_ld{j}") for j in range(2)]
    sstq = [ctx.sem(tag + f"_st{j}") for j in range(2)]
    sdone = ctx.sem(tag + "_done")
    pch = ctx.chain(tag + "_p")

    ssc = ctx.sem(tag + "_sc")
    vq = ctx.chain(tag + "_vq")

    for i in range(NT):
        j = i % 2

        @ctx.blk.gpsimd
        def _(g, i=i, j=j):
            if i >= 2:
                g.wait_ge(sdone.h, i - 1)
            sld[j].inc(g.dma_start(out=wb[:, j * C:(j + 1) * C],
                                   in_=w_dram[i * 128:(i + 1) * 128, :]), 16)

        @ctx.blk.vector
        def _(v, i=i, j=j):
            v.wait_ge(sld[j].h, (i // 2 + 1) * 16)
            if i > 0:
                v.wait_ge(sdone.h, i)
            w = wb[:, j * C:(j + 1) * C]
            vq.step(v, v.reduce_sum(sc[:], w.rearrange("p (g k) -> p g k", k=64),
                                    axis=AX.X, apply_absolute_value=True))
            vq.step(v, v.tensor_scalar(scb[:], sc[:], 1.0 / 64, None, ALU.mult))
            vq.step(v, v.tensor_scalar(scb[:], scb[:], 1e-8, None, ALU.max))
            vq.step(v, v.tensor_copy(sce[:].rearrange("p (g k) -> p g k", k=64),
                                     scb[:].rearrange("p (g o) -> p g o", o=1).broadcast_to([128, G, 64])))
            vq.step(v, v.reciprocal(rcp[:], scb[:]))
            vq.step(v, v.tensor_copy(rce[:].rearrange("p (g k) -> p g k", k=64),
                                     rcp[:].rearrange("p (g o) -> p g o", o=1).broadcast_to([128, G, 64])))
            ssc.inc(v.nop(), 1)

        @ctx.blk.gpsimd
        def _(g, i=i, j=j):
            g.wait_ge(ssc.h, i + 1)
            w = wb[:, j * C:(j + 1) * C]
            pch.step(g, g.tensor_tensor(qd[:], w, rce[:], ALU.mult))
            pch.step(g, g.tensor_scalar(qd[:], qd[:], RNE_C, RNE_C, ALU.add, ALU.subtract))
            pch.step(g, g.tensor_scalar(qd[:], qd[:], 1.0, -1.0, ALU.min, ALU.max))
            if i >= 2:
                g.wait_ge(sstq[j].h, (i // 2) * 16)
            pch.step(g, g.tensor_tensor(ot[:, j * C:(j + 1) * C], qd[:], sce[:], ALU.mult))
            sdone.inc(g.nop(), 1)
            sstq[j].inc(g.dma_start(out=out_dram[i * 128:(i + 1) * 128, :],
                                    in_=ot[:, j * C:(j + 1) * C]), 16)
    return sstq


def emit_transpose_mat(ctx, tag, src_dram, dst_dram, R, C, gate_sems=()):
    nc = ctx.nc
    NT = C // 128
    tb = nc.alloc_sbuf_tensor(tag + "_tb", [128, 2 * R], BF16)
    sld = [ctx.sem(tag + f"_ld{j}") for j in range(2)]
    sstq = [ctx.sem(tag + f"_st{j}") for j in range(2)]

    @ctx.blk.sync
    def _(sync):
        for s in gate_sems:
            s.wait(sync)
        for i in range(NT):
            j = i % 2
            if i >= 2:
                sync.wait_ge(sstq[j].h, (i // 2) * 16)
            sld[j].inc(sync.dma_start(out=tb[:, j * R:(j + 1) * R],
                                      in_=src_dram[:, i * 128:(i + 1) * 128], transpose=True), 16)
            sync.wait_ge(sld[j].h, (i // 2 + 1) * 16)
            sstq[j].inc(sync.dma_start(out=dst_dram[i * 128:(i + 1) * 128, :],
                                       in_=tb[:, j * R:(j + 1) * R]), 16)
    return sstq


def emit_ffn0(ctx, tag, xn_dram, w1T_dram, w2T_dram, m0_dram):
    nc = ctx.nc
    H = 8192
    NHC = H // 128
    FEPS = float(np.finfo(np.float32).eps)
    sxl = [ctx.sem(tag + f"_xl{j}") for j in range(2)]
    sw1 = [ctx.sem(tag + f"_w1{j}") for j in range(2)]
    sw2 = [ctx.sem(tag + f"_w2{j}") for j in range(2)]
    sps = ctx.sem(tag + "_ps")
    sev = ctx.sem(tag + "_ev")
    sprow = ctx.sem(tag + "_prow")
    srowc = ctx.sem(tag + "_rowc")
    sact = ctx.sem(tag + "_act")
    srstd = ctx.sem(tag + "_rstd")
    shn = ctx.sem(tag + "_hn")
    sstm = [ctx.sem(tag + f"_st{j}") for j in range(2)]

    SXN = NDC * 512
    SW1 = NDC * 512
    SW2 = NHC * 256
    xns = nc.alloc_sbuf_tensor(tag + "_xns", [128, 2 * SXN], BF16)
    w1s = nc.alloc_sbuf_tensor(tag + "_w1s", [128, 2 * SW1], BF16)
    w2s = nc.alloc_sbuf_tensor(tag + "_w2s", [128, 2 * SW2], BF16)
    hbf = nc.alloc_sbuf_tensor(tag + "_h", [128, NHC * 512], BF16)
    tr = nc.alloc_sbuf_tensor(tag + "_tr", [128, 2 * 512], F32)
    hsq = nc.alloc_sbuf_tensor(tag + "_hsq", [128, 2 * 512], BF16)
    rowb = nc.alloc_sbuf_tensor(tag + "_rowb", [1, 512], F32)
    rowbb = nc.alloc_sbuf_tensor(tag + "_rowbb", [1, 512], BF16)
    bcst = nc.alloc_sbuf_tensor(tag + "_bc", [128, 512], BF16)
    ob = nc.alloc_sbuf_tensor(tag + "_ob", [128, 2 * 512], BF16)
    mm = ctx.ps["mm"]
    prow = ctx.ps["row"]
    vch = ctx.chain(tag + "_v")
    sch = ctx.chain(tag + "_s")
    crdy = ctx.sem("const_rdy")

    @ctx.blk.sync
    def _(sync):
        for tb in range(2):
            srcx = xn_dram.rearrange("(c p) t -> p c t", p=128)[:, :, tb * 512:(tb + 1) * 512]
            dstx = xns[:, (tb % 2) * SXN:(tb % 2 + 1) * SXN].rearrange("p (c t) -> p c t", t=512)
            sxl[tb % 2].inc(sync.dma_start(out=dstx, in_=srcx), 16)

    @ctx.blk.gpsimd
    def _(g):
        for tb in range(4):
            for hb in range(16):
                nw = tb * 16 + hb
                if nw >= 2:
                    tb2, hb2 = divmod(nw - 2, 16)
                    g.wait_ge(sps.h, tb2 * 81 + (hb2 + 1) * 4)
                srcw = w1T_dram.rearrange("(k p) h -> p k h", p=128)[:, :, hb * 512:(hb + 1) * 512]
                dstw = w1s[:, (nw % 2) * SW1:(nw % 2 + 1) * SW1].rearrange("p (k h) -> p k h", h=512)
                sw1[nw % 2].inc(g.dma_start(out=dstw, in_=srcw), 16)
            for db in range(8):
                nw = tb * 8 + db
                if nw >= 2:
                    tb2, db2 = divmod(nw - 2, 8)
                    g.wait_ge(sps.h, tb2 * 81 + 65 + (db2 + 1) * 2)
                srcw = w2T_dram.rearrange("(k p) d -> p k d", p=128)[:, :, db * 256:(db + 1) * 256]
                dstw = w2s[:, (nw % 2) * SW2:(nw % 2 + 1) * SW2].rearrange("p (k d) -> p k d", d=256)
                sw2[nw % 2].inc(g.dma_start(out=dstw, in_=srcw), 16)

    def emit_prow(t, tb, hc):
        m = tb * NHC + hc
        t.wait_ge(sact.h, m + 1)
        crdy.wait(t)
        if hc == 0:
            t.wait_ge(srowc.h, tb)
        ins = t.matmul(prow[:], ctx.sb["ones_col_bf"][:],
                       hsq[:, (m % 2) * 512:(m % 2) * 512 + 512],
                       start=(hc == 0), stop=(hc == NHC - 1))
        sprow.inc(ins, 1)

    npt = 0
    for tb in range(4):
        xbase = (tb % 2) * SXN
        for hc in range(NHC):
            hb, hc4 = divmod(hc, 4)
            n = npt; npt += 1
            m = tb * NHC + hc

            @ctx.blk.tensor
            def _(t, tb=tb, hb=hb, hc4=hc4, hc=hc, n=n, m=m, xbase=xbase):
                if hc == 0:
                    t.wait_ge(sxl[tb % 2].h, (tb // 2 + 1) * 16)
                if hc4 == 0:
                    t.wait_ge(sw1[(tb * 16 + hb) % 2].h, ((tb * 16 + hb) // 2 + 1) * 16)
                if n >= 4:
                    t.wait_ge(sev.h, n - 3)
                base = ((tb * 16 + hb) % 2) * SW1
                for k in range(NDC):
                    ins = t.matmul(mm[n % 4][:],
                                   w1s[:, base + k * 512 + hc4 * 128: base + k * 512 + hc4 * 128 + 128],
                                   xns[:, xbase + k * 512: xbase + (k + 1) * 512],
                                   start=(k == 0), stop=(k == NDC - 1))
                sps.inc(ins, 1)
                if hc >= 2:
                    emit_prow(t, tb, hc - 2)
                if hc == NHC - 1:
                    emit_prow(t, tb, hc - 1)
                    emit_prow(t, tb, hc)

            @ctx.blk.vector
            def _(v, n=n, m=m):
                v.wait_ge(sps.h, n + 1)
                if m >= 2:
                    v.wait_ge(sact.h, m - 1)
                vch.step(v, v.tensor_scalar(tr[:, (m % 2) * 512:(m % 2) * 512 + 512],
                                            mm[n % 4][:], 0.0, None, ALU.max))
                sev.inc(v.nop(), 1)

            @ctx.blk.scalar
            def _(s, tb=tb, hc=hc, n=n, m=m):
                s.wait_ge(sev.h, n + 1)
                if hc == 0 and tb > 0:
                    s.wait_ge(sps.h, tb * 81)
                if m >= 2:
                    s.wait_ge(sprow.h, m - 1)
                sch.step(s, s.activation(hbf[:, hc * 512:(hc + 1) * 512],
                                         tr[:, (m % 2) * 512:(m % 2) * 512 + 512], AF.Square))
                sch.step(s, s.activation(hsq[:, (m % 2) * 512:(m % 2) * 512 + 512],
                                         hbf[:, hc * 512:(hc + 1) * 512], AF.Square))
                sact.inc(s.nop(), 1)

        if tb + 2 < 4:
            @ctx.blk.sync
            def _(sync, tb=tb):
                sync.wait_ge(sps.h, tb * 81 + 64)
                srcx = xn_dram.rearrange("(c p) t -> p c t", p=128)[:, :, (tb + 2) * 512:(tb + 3) * 512]
                dstx = xns[:, (tb % 2) * SXN:(tb % 2 + 1) * SXN].rearrange("p (c t) -> p c t", t=512)
                sxl[tb % 2].inc(sync.dma_start(out=dstx, in_=srcx), 16)

        @ctx.blk.vector
        def _(v, tb=tb):
            v.wait_ge(sprow.h, (tb + 1) * NHC)
            vch.step(v, v.tensor_copy(rowb[:], prow[:]))
            srowc.inc(v.nop(), 1)

        @ctx.blk.scalar
        def _(s, tb=tb):
            s.wait_ge(srowc.h, tb + 1)
            sch.step(s, s.activation(rowb[:], rowb[:], AF.Ln, bias=ctx.sb["misc"][0:1, 0:1],
                                     scale=ctx.sb["misc"][0:1, 1:2]))
            sch.step(s, s.activation(rowbb[:], rowb[:], AF.Exp, scale=-0.5))
            srstd.inc(s.nop(), 1)

        n = npt; npt += 1

        @ctx.blk.tensor
        def _(t, tb=tb, n=n):
            t.wait_ge(srstd.h, tb + 1)
            if n >= 4:
                t.wait_ge(sev.h, n - 3)
            sps.inc(t.matmul(mm[n % 4][:], ctx.sb["ones_row_bf"][:], rowbb[:],
                             start=True, stop=True), 1)

        @ctx.blk.vector
        def _(v, tb=tb, n=n):
            v.wait_ge(sps.h, n + 1)
            vch.step(v, v.tensor_copy(bcst[:], mm[n % 4][:]))
            for hc in range(NHC):
                ins = v.tensor_mul(hbf[:, hc * 512:(hc + 1) * 512],
                                   hbf[:, hc * 512:(hc + 1) * 512], bcst[:])
            vch.sem.inc(ins, 1)
            v.wait_ge(vch.sem.h, vch.sem.n)
            sev.inc(v.nop(), 1)
            shn.inc(v.nop(), 1)

        for dc in range(16):
            db, dc2 = divmod(dc, 2)
            n = npt; npt += 1
            bt = tb * 16 + dc

            @ctx.blk.tensor
            def _(t, tb=tb, db=db, dc2=dc2, n=n):
                if dc2 == 0:
                    t.wait_ge(sw2[(tb * 8 + db) % 2].h, ((tb * 8 + db) // 2 + 1) * 16)
                t.wait_ge(shn.h, tb + 1)
                if n >= 4:
                    t.wait_ge(sev.h, n - 3)
                base = ((tb * 8 + db) % 2) * SW2
                for k in range(NHC):
                    ins = t.matmul(mm[n % 4][:],
                                   w2s[:, base + k * 256 + dc2 * 128: base + k * 256 + dc2 * 128 + 128],
                                   hbf[:, k * 512:(k + 1) * 512],
                                   start=(k == 0), stop=(k == NHC - 1))
                sps.inc(ins, 1)

            @ctx.blk.vector
            def _(v, n=n, bt=bt):
                v.wait_ge(sps.h, n + 1)
                if bt >= 2:
                    v.wait_ge(sstm[bt % 2].h, (bt // 2) * 16)
                sev.inc(v.tensor_copy(ob[:, (bt % 2) * 512:(bt % 2) * 512 + 512], mm[n % 4][:]), 1)

            @ctx.blk.sync
            def _(sync, tb=tb, dc=dc, n=n, bt=bt):
                sync.wait_ge(sev.h, n + 1)
                dst = m0_dram[dc * 128:(dc + 1) * 128, tb * 512:(tb + 1) * 512]
                sstm[bt % 2].inc(sync.dma_start(out=dst, in_=ob[:, (bt % 2) * 512:(bt % 2) * 512 + 512]), 16)
    return sstm


def emit_proj_fm(ctx, tag, srcT_sb, w_dram, out_dram, D_in, D_out, src_rdy, ntok=T):
    nc = ctx.nc
    NKC = D_in // 128
    NOC = D_out // 128
    NTB = ntok // 512
    sw = [ctx.sem(tag + f"_w{j}") for j in range(2)]
    sps = ctx.sem(tag + "_ps")
    sev = ctx.sem(tag + "_ev")
    sstq = [ctx.sem(tag + f"_st{j}") for j in range(2)]
    ws = nc.alloc_sbuf_tensor(tag + "_ws", [128, 2 * NKC * 128], BF16)
    ob = nc.alloc_sbuf_tensor(tag + "_ob", [128, 2 * 512], BF16)
    mm = ctx.ps["mm"]

    @ctx.blk.gpsimd
    def _(g):
        for oc in range(NOC):
            if oc >= 2:
                g.wait_ge(sps.h, (oc - 1) * NTB)
            srcw = w_dram.rearrange("(k p) o -> p k o", p=128)[:, :, oc * 128:(oc + 1) * 128]
            dstw = ws[:, (oc % 2) * NKC * 128:(oc % 2 + 1) * NKC * 128].rearrange("p (k o) -> p k o", o=128)
            sw[oc % 2].inc(g.dma_start(out=dstw, in_=srcw), 16)

    n = 0
    for oc in range(NOC):
        for tbk in range(NTB):
            nn = n; n += 1

            @ctx.blk.tensor
            def _(t, oc=oc, tbk=tbk, nn=nn):
                if tbk == 0:
                    t.wait_ge(sw[oc % 2].h, (oc // 2 + 1) * 16)
                if nn == 0:
                    src_rdy.wait(t)
                if nn >= 4:
                    t.wait_ge(sev.h, nn - 3)
                base = (oc % 2) * NKC * 128
                for k in range(NKC):
                    ins = t.matmul(mm[nn % 4][:],
                                   ws[:, base + k * 128: base + (k + 1) * 128],
                                   srcT_sb[:, k * ntok + tbk * 512: k * ntok + tbk * 512 + 512],
                                   start=(k == 0), stop=(k == NKC - 1))
                sps.inc(ins, 1)

            @ctx.blk.vector
            def _(v, nn=nn):
                v.wait_ge(sps.h, nn + 1)
                if nn >= 2:
                    v.wait_ge(sstq[nn % 2].h, (nn // 2) * 16)
                sev.inc(v.tensor_copy(ob[:, (nn % 2) * 512:(nn % 2) * 512 + 512], mm[nn % 4][:]), 1)

            @ctx.blk.sync
            def _(sync, oc=oc, tbk=tbk, nn=nn):
                sync.wait_ge(sev.h, nn + 1)
                dst = out_dram[oc * 128:(oc + 1) * 128, tbk * 512:(tbk + 1) * 512]
                sstq[nn % 2].inc(sync.dma_start(out=dst, in_=ob[:, (nn % 2) * 512:(nn % 2) * 512 + 512]), 16)
    return sstq


def emit_attn(ctx, tag, q_dram, k_blocks, v_blocks, lq1_d, lk1_d, lq2_d, lk2_d,
              rms_d, wproj_dram, a0_dram):
    nc = ctx.nc
    TK = sum(r for (_, r) in k_blocks)
    NKT = TK // 128
    NQT = T // 128
    mm = ctx.ps["mm"]
    trb = ctx.ps["trb"]
    crdy = ctx.sem("const_rdy")

    kt1 = nc.alloc_sbuf_tensor(tag + "_kt1", [128, 2 * TK], BF16)
    kt2 = nc.alloc_sbuf_tensor(tag + "_kt2", [128, 2 * TK], BF16)
    qt1 = nc.alloc_sbuf_tensor(tag + "_qt1", [128, 2 * T], BF16)
    qt2 = nc.alloc_sbuf_tensor(tag + "_qt2", [128, 2 * T], BF16)
    vsb = nc.alloc_sbuf_tensor(tag + "_v", [128, 2 * 2 * NKT * 129], BF16)
    expb = nc.alloc_sbuf_tensor(tag + "_exp", [128, 2 * 512], BF16)
    t1b = nc.alloc_sbuf_tensor(tag + "_t1", [128, 4 * 128], F32)
    abuf = nc.alloc_sbuf_tensor(tag + "_a", [128, 128], F32)
    nbf = nc.alloc_sbuf_tensor(tag + "_nbf", [128, 128], BF16)
    sqs = nc.alloc_sbuf_tensor(tag + "_sqs", [128, 128], F32)
    colsc = nc.alloc_sbuf_tensor(tag + "_cols", [128, 8], F32)
    lamneg = nc.alloc_sbuf_tensor(tag + "_lamneg", [128, 1], F32)
    rmsb = nc.alloc_sbuf_tensor(tag + "_rmsb", [128, 128], F32)
    lrow = nc.alloc_sbuf_tensor(tag + "_lrow", [1, 400], F32)
    lam_bf = nc.alloc_sbuf_tensor(tag + "_lambf", [1, 1], BF16)
    rmsrow = nc.alloc_sbuf_tensor(tag + "_rmsrow", [1, 128], F32)
    rmsrow_bf = nc.alloc_sbuf_tensor(tag + "_rmsrowb", [1, 128], BF16)
    normT = nc.alloc_sbuf_tensor(tag + "_normT", [128, NDC * T], BF16)

    sprl = [ctx.sem(tag + f"_prl{j}") for j in range(2)]
    smset = ctx.sem(tag + "_mset")
    ssc = ctx.sem(tag + "_sc")
    sexp = ctx.sem(tag + "_exp")
    spv = ctx.sem(tag + "_pv")
    sev1 = ctx.sem(tag + "_ev1")
    sep = ctx.sem(tag + "_ep")
    str_ = ctx.sem(tag + "_tr")
    sevn = ctx.sem(tag + "_evn")
    slam = ctx.sem(tag + "_lam")
    vch = ctx.chain(tag + "_v")
    sch = ctx.chain(tag + "_s")

    @ctx.blk.sync
    def _(sync):
        slam.inc(sync.dma_start(out=lrow[:, 0:64], in_=lq1_d), 16)
        slam.inc(sync.dma_start(out=lrow[:, 64:128], in_=lk1_d), 16)
        slam.inc(sync.dma_start(out=lrow[:, 128:192], in_=lq2_d), 16)
        slam.inc(sync.dma_start(out=lrow[:, 192:256], in_=lk2_d), 16)
        slam.inc(sync.dma_start(out=rmsrow[:], in_=rms_d), 16)

    @ctx.blk.vector
    def _(v):
        slam.wait(v)
        vch.step(v, v.tensor_mul(lrow[:, 256:320], lrow[:, 0:64], lrow[:, 64:128]))
        vch.step(v, v.reduce_sum(lrow[:, 384:385], lrow[:, 256:320], axis=AX.X))
        vch.step(v, v.tensor_mul(lrow[:, 256:320], lrow[:, 128:192], lrow[:, 192:256]))
        vch.step(v, v.reduce_sum(lrow[:, 385:386], lrow[:, 256:320], axis=AX.X))
        sep.inc(v.nop(), 1)

    @ctx.blk.scalar
    def _(s):
        s.wait_ge(sep.h, 1)
        sch.step(s, s.activation(lrow[:, 384:385], lrow[:, 384:385], AF.Exp))
        sch.step(s, s.activation(lrow[:, 385:386], lrow[:, 385:386], AF.Exp))
        sep.inc(s.nop(), 1)

    @ctx.blk.vector
    def _(v):
        v.wait_ge(sep.h, 2)
        vch.step(v, v.tensor_sub(lrow[:, 386:387], lrow[:, 385:386], lrow[:, 384:385]))
        vch.step(v, v.tensor_scalar(lrow[:, 386:387], lrow[:, 386:387], -LAMBDA_INIT, None, ALU.add))
        vch.step(v, v.tensor_copy(lam_bf[:], lrow[:, 386:387]))
        vch.step(v, v.tensor_scalar(rmsrow[:], rmsrow[:], 1.0 - LAMBDA_INIT, None, ALU.mult))
        vch.step(v, v.tensor_copy(rmsrow_bf[:], rmsrow[:]))
        sep.inc(v.nop(), 1)

    @ctx.blk.tensor
    def _(t):
        t.wait_ge(sep.h, 3)
        crdy.wait(t)
        t.matmul(mm[0][:, 0:1], ctx.sb["ones_row_bf"][:], lam_bf[:], start=True, stop=True)
        ins = t.matmul(mm[1][:, 0:128], ctx.sb["ones_row_bf"][:], rmsrow_bf[:], start=True, stop=True)
        sep.inc(ins, 1)

    @ctx.blk.vector
    def _(v):
        v.wait_ge(sep.h, 4)
        vch.step(v, v.tensor_copy(lamneg[:], mm[0][:, 0:1]))
        vch.step(v, v.tensor_copy(rmsb[:], mm[1][:, 0:128]))
        sep.inc(v.nop(), 1)
    SEP0 = 5

    def issue_pair(sync, p):
        j = p % 2
        ofs = 0
        for (blk_ap, rows) in k_blocks:
            sprl[j].inc(sync.dma_start(out=kt1[:, j * TK + ofs: j * TK + ofs + rows],
                                       in_=blk_ap[:, p * 128:(p + 1) * 128], transpose=True), 16)
            sprl[j].inc(sync.dma_start(out=kt2[:, j * TK + ofs: j * TK + ofs + rows],
                                       in_=blk_ap[:, 1024 + p * 128: 1024 + (p + 1) * 128], transpose=True), 16)
            ofs += rows
        sprl[j].inc(sync.dma_start(out=qt1[:, j * T: (j + 1) * T],
                                   in_=q_dram[:, p * 128:(p + 1) * 128], transpose=True), 16)
        sprl[j].inc(sync.dma_start(out=qt2[:, j * T: (j + 1) * T],
                                   in_=q_dram[:, 1024 + p * 128: 1024 + (p + 1) * 128], transpose=True), 16)
        for hh in range(2):
            ofs = 0
            for (vblk_ap, rows) in v_blocks:
                dst = vsb[:, (j * 2 + hh) * NKT * 129:(j * 2 + hh + 1) * NKT * 129]
                dst = dst.rearrange("p (a e) -> p a e", e=129)[:, ofs // 128:(ofs + rows) // 128, 0:128]
                srcv = vblk_ap[2 * p + hh].rearrange("(a p) d -> p a d", p=128)
                sprl[j].inc(sync.dma_start(out=dst, in_=srcv), 16)
                ofs += rows
    NDMA_PAIR = 2 * len(k_blocks) + 2 + 2 * len(v_blocks)

    @ctx.blk.vector
    def _(v):
        smset.inc(v.memset(vsb[:, 0:2 * NKT * 129], 1.0), 1)
        smset.inc(v.memset(vsb[:, 2 * NKT * 129:4 * NKT * 129], 1.0), 1)

    @ctx.blk.sync
    def _(sync):
        sync.wait_ge(smset.h, 1)
        issue_pair(sync, 0)
        sync.wait_ge(smset.h, 2)
        issue_pair(sync, 1)

    def emit_pv(t, kt, nk, vbase, first, last):
        t.wait_ge(sexp.h, nk + 1)
        for jj in range(4):
            ins = t.matmul(mm[2 + jj // 2][:, (jj % 2) * 256:(jj % 2) * 256 + 129],
                           expb[:, (nk % 2) * 512 + jj * 128:(nk % 2) * 512 + (jj + 1) * 128],
                           vsb[:, vbase + kt * 129: vbase + (kt + 1) * 129],
                           start=(first and jj % 2 == 0), stop=last)
        spv.inc(ins, 1)

    nkc = [0]
    nac = [0]
    nqc = [0]
    for p in range(8):
        j = p % 2
        for hh in range(2):
            h = 2 * p + hh
            hb = hh * 64
            vbase = (j * 2 + hh) * NKT * 129
            for qb in range(4):
                for st_i in range(2):
                    ktb, qtb = (kt1, qt1) if st_i == 0 else (kt2, qt2)
                    na = nac[0]; nac[0] += 1
                    for kt in range(NKT):
                        nk = nkc[0]; nkc[0] += 1

                        @ctx.blk.tensor
                        def _(t, p=p, j=j, hb=hb, qb=qb, st_i=st_i, kt=kt, nk=nk, na=na,
                              ktb=ktb, qtb=qtb, vbase=vbase, nq0=nqc[0]):
                            if kt == 0:
                                if st_i == 0 and qb == 0 and hb == 0:
                                    t.wait_ge(sprl[j].h, (p // 2 + 1) * NDMA_PAIR * 16)
                                    if p == 0:
                                        sep.wait(t, SEP0)
                                if st_i == 0 and nq0 > 0:
                                    t.wait_ge(sep.h, SEP0 + 3 * nq0)
                                if st_i == 1:
                                    t.wait_ge(sev1.h, (na + 1) // 2)
                            if nk >= 2:
                                t.wait_ge(sexp.h, nk - 1)
                            ins = t.matmul(mm[nk % 2][:],
                                           ktb[hb:hb + 64, j * TK + kt * 128: j * TK + (kt + 1) * 128],
                                           qtb[hb:hb + 64, j * T + qb * 512: j * T + qb * 512 + 512],
                                           start=True, stop=True)
                            ssc.inc(ins, 1)
                            if kt >= 1:
                                emit_pv(t, kt - 1, nk - 1, vbase, kt - 1 == 0, False)
                            if kt == NKT - 1:
                                emit_pv(t, kt, nk, vbase, False, True)

                        @ctx.blk.scalar
                        def _(s, nk=nk):
                            s.wait_ge(ssc.h, nk + 1)
                            if nk >= 2:
                                s.wait_ge(spv.h, nk - 1)
                            sch.step(s, s.activation(expb[:, (nk % 2) * 512:(nk % 2) * 512 + 512],
                                                     mm[nk % 2][:], AF.Exp, scale=0.125))
                            sexp.inc(s.nop(), 1)

                    if st_i == 0:
                        @ctx.blk.vector
                        def _(v, na=na):
                            v.wait_ge(spv.h, (na + 1) * NKT)
                            for jj in range(4):
                                ob_ = mm[2 + jj // 2]
                                col = (jj % 2) * 256
                                vch.step(v, v.reciprocal(colsc[:, 0:1], ob_[:, col + 128:col + 129]))
                                vch.step(v, v.tensor_scalar(t1b[:, jj * 128:(jj + 1) * 128],
                                                            ob_[:, col:col + 128], colsc[:, 0:1],
                                                            None, ALU.mult))
                            sev1.inc(v.nop(), 1)
                    else:
                        for jj in range(4):
                            nq = nqc[0]; nqc[0] += 1

                            @ctx.blk.vector
                            def _(v, na=na, jj=jj, nq=nq):
                                if jj == 0:
                                    v.wait_ge(spv.h, (na + 1) * NKT)
                                if nq >= 1:
                                    v.wait_ge(sep.h, SEP0 + 3 * (nq - 1) + 2)
                                ob_ = mm[2 + jj // 2]
                                col = (jj % 2) * 256
                                vch.step(v, v.reciprocal(colsc[:, 1:2], ob_[:, col + 128:col + 129]))
                                vch.step(v, v.tensor_mul(colsc[:, 2:3], colsc[:, 1:2], lamneg[:]))
                                vch.step(v, v.scalar_tensor_tensor(abuf[:], ob_[:, col:col + 128],
                                                                   colsc[:, 2:3], t1b[:, jj * 128:(jj + 1) * 128],
                                                                   ALU.mult, ALU.add))
                                sep.inc(v.nop(), 1)

                            @ctx.blk.scalar
                            def _(s, nq=nq):
                                s.wait_ge(sep.h, SEP0 + 3 * nq + 1)
                                sch.step(s, s.activation(sqs[:], abuf[:], AF.Square, accum_out=colsc[:, 3:4]))
                                sch.step(s, s.activation(colsc[:, 4:5], colsc[:, 3:4], AF.Ln,
                                                         bias=ctx.sb["misc_col"][:, 2:3], scale=1.0 / 128))
                                sch.step(s, s.activation(colsc[:, 4:5], colsc[:, 4:5], AF.Exp, scale=-0.5))
                                sep.inc(s.nop(), 1)

                            @ctx.blk.vector
                            def _(v, nq=nq):
                                v.wait_ge(sep.h, SEP0 + 3 * nq + 2)
                                if nq >= 1:
                                    v.wait_ge(str_.h, nq)
                                vch.step(v, v.tensor_scalar(abuf[:], abuf[:], colsc[:, 4:5], None, ALU.mult))
                                vch.step(v, v.tensor_mul(abuf[:], abuf[:], rmsb[:]))
                                vch.step(v, v.tensor_copy(nbf[:], abuf[:]))
                                sep.inc(v.nop(), 1)

                            @ctx.blk.tensor
                            def _(t, nq=nq):
                                t.wait_ge(sep.h, SEP0 + 3 * nq + 3)
                                if nq >= 2:
                                    t.wait_ge(sevn.h, nq - 1)
                                str_.inc(t.transpose(trb[nq % 2][:, 0:128], nbf[:],
                                                     ctx.sb["ident_bf"][:]), 1)

                            @ctx.blk.vector
                            def _(v, nq=nq, h=h, qb=qb, jj=jj):
                                v.wait_ge(str_.h, nq + 1)
                                qtile = qb * 4 + jj
                                sevn.inc(v.tensor_copy(
                                    normT[:, h * T + qtile * 128: h * T + (qtile + 1) * 128],
                                    trb[nq % 2][:, 0:128]), 1)

        if p + 2 < 8:
            @ctx.blk.vector
            def _(v, p=p, j=j):
                v.wait_ge(spv.h, (p + 1) * 2 * 4 * 2 * NKT)
                smset.inc(v.memset(vsb[:, j * 2 * NKT * 129:(j + 1) * 2 * NKT * 129], 1.0), 1)

            @ctx.blk.sync
            def _(sync, p=p, j=j):
                sync.wait_ge(smset.h, p + 3)
                issue_pair(sync, p + 2)

    srdy = ctx.sem(tag + "_nrm_rdy")

    @ctx.blk.vector
    def _(v):
        v.wait_ge(sevn.h, 16 * NQT)
        srdy.inc(v.nop(), 1)

    return emit_proj_fm(ctx, tag + "_wp", normT, wproj_dram, a0_dram, D, D, srdy)


def emit_proj_g(ctx, tag, src_dram, wT_dram, outs, D_in, ntok=T, gate_sems=()):
    nc = ctx.nc
    KC = D_in // 128
    D_out = sum(o[2] for o in outs)
    ocs = []
    for o in outs:
        od, r0, nr = o[0], o[1], o[2]
        odt = o[3] if len(o) > 3 else BF16
        r = 0
        while r < nr:
            m = min(128, nr - r)
            ocs.append((od, r0 + r, m, odt))
            r += m
    NOC = len(ocs)
    any_f32 = any(oc[3] == F32 for oc in ocs)
    NTB = ntok // 512
    sw = [ctx.sem(tag + f"_w{j}") for j in range(2)]
    sx = [ctx.sem(tag + f"_x{j}") for j in range(2)]
    sps = ctx.sem(tag + "_ps")
    sev = ctx.sem(tag + "_ev")
    sstq = [ctx.sem(tag + f"_st{j}") for j in range(2)]
    resident = (D_in <= 2048)
    if resident:
        xs = nc.alloc_sbuf_tensor(tag + "_xs", [128, KC * ntok], BF16)
    else:
        xs = nc.alloc_sbuf_tensor(tag + "_xs", [128, 2 * KC * 512], BF16)
    ws = nc.alloc_sbuf_tensor(tag + "_ws", [128, 2 * KC * 128], BF16)
    ob = nc.alloc_sbuf_tensor(tag + "_ob", [128, 2 * 512], BF16)
    obf = nc.alloc_sbuf_tensor(tag + "_obf", [32, 2 * 512], F32) if any_f32 else None
    mm = ctx.ps["mm"]

    if resident:
        @ctx.blk.sync
        def _(sync):
            for s in gate_sems:
                s.wait(sync)
            srcx = src_dram.rearrange("(c p) t -> p c t", p=128)
            sx[0].inc(sync.dma_start(out=xs[:].rearrange("p (c t) -> p c t", t=ntok), in_=srcx), 16)

        @ctx.blk.gpsimd
        def _(g):
            for oc in range(NOC):
                if oc >= 2:
                    g.wait_ge(sps.h, (oc - 1) * NTB)
                col0 = sum(o2[2] for o2 in ocs[:oc])
                m = ocs[oc][2]
                srcw = wT_dram.rearrange("(k p) o -> p k o", p=128)[:, :, col0:col0 + m]
                dstw = ws[:, (oc % 2) * KC * 128:(oc % 2) * KC * 128 + KC * m].rearrange(
                    "p (k o) -> p k o", o=m)
                sw[oc % 2].inc(g.dma_start(out=dstw, in_=srcw), 16)

        n = 0
        for oc in range(NOC):
            od, lr0, m, odt = ocs[oc]
            for tbk in range(NTB):
                nn = n; n += 1
                buf = obf if odt == F32 else ob

                @ctx.blk.tensor
                def _(t, oc=oc, tbk=tbk, nn=nn, m=m):
                    if tbk == 0:
                        t.wait_ge(sw[oc % 2].h, (oc // 2 + 1) * 16)
                    if nn == 0:
                        t.wait_ge(sx[0].h, 16)
                    if nn >= 4:
                        t.wait_ge(sev.h, nn - 3)
                    base = (oc % 2) * KC * 128
                    for k in range(KC):
                        ins = t.matmul(mm[nn % 4][0:m, :],
                                       ws[:, base + k * m: base + (k + 1) * m],
                                       xs[:, k * ntok + tbk * 512: k * ntok + tbk * 512 + 512],
                                       start=(k == 0), stop=(k == KC - 1))
                    sps.inc(ins, 1)

                @ctx.blk.vector
                def _(v, nn=nn, m=m, buf=buf):
                    v.wait_ge(sps.h, nn + 1)
                    if nn >= 2:
                        v.wait_ge(sstq[nn % 2].h, (nn // 2) * 16)
                    sev.inc(v.tensor_copy(buf[0:m, (nn % 2) * 512:(nn % 2) * 512 + 512],
                                          mm[nn % 4][0:m, :]), 1)

                @ctx.blk.sync
                def _(sync, od=od, lr0=lr0, m=m, tbk=tbk, nn=nn, buf=buf):
                    sync.wait_ge(sev.h, nn + 1)
                    dst = od[lr0:lr0 + m, tbk * 512:(tbk + 1) * 512]
                    sstq[nn % 2].inc(sync.dma_start(out=dst, in_=buf[0:m, (nn % 2) * 512:(nn % 2) * 512 + 512]), 16)
        return sstq
    else:
        @ctx.blk.sync
        def _(sync):
            for s in gate_sems:
                s.wait(sync)
            for tbk in range(min(2, NTB)):
                srcx = src_dram.rearrange("(c p) t -> p c t", p=128)[:, :, tbk * 512:(tbk + 1) * 512]
                dstx = xs[:, (tbk % 2) * KC * 512:(tbk % 2 + 1) * KC * 512].rearrange("p (c t) -> p c t", t=512)
                sx[tbk % 2].inc(sync.dma_start(out=dstx, in_=srcx), 16)

        @ctx.blk.gpsimd
        def _(g):
            for tbk in range(NTB):
                for oc in range(NOC):
                    nw = tbk * NOC + oc
                    if nw >= 2:
                        g.wait_ge(sps.h, nw - 1)
                    col0 = sum(o2[2] for o2 in ocs[:oc])
                    m = ocs[oc][2]
                    srcw = wT_dram.rearrange("(k p) o -> p k o", p=128)[:, :, col0:col0 + m]
                    dstw = ws[:, (nw % 2) * KC * 128:(nw % 2) * KC * 128 + KC * m].rearrange(
                        "p (k o) -> p k o", o=m)
                    sw[nw % 2].inc(g.dma_start(out=dstw, in_=srcw), 16)

        n = 0
        for tbk in range(NTB):
            for oc in range(NOC):
                od, lr0, m, _odt = ocs[oc]
                nn = n; n += 1

                @ctx.blk.tensor
                def _(t, oc=oc, tbk=tbk, nn=nn, m=m):
                    t.wait_ge(sw[nn % 2].h, (nn // 2 + 1) * 16)
                    if oc == 0:
                        t.wait_ge(sx[tbk % 2].h, (tbk // 2 + 1) * 16)
                    if nn >= 4:
                        t.wait_ge(sev.h, nn - 3)
                    base = (nn % 2) * KC * 128
                    xbase = (tbk % 2) * KC * 512
                    for k in range(KC):
                        ins = t.matmul(mm[nn % 4][0:m, :],
                                       ws[:, base + k * m: base + (k + 1) * m],
                                       xs[:, xbase + k * 512: xbase + (k + 1) * 512],
                                       start=(k == 0), stop=(k == KC - 1))
                    sps.inc(ins, 1)

                @ctx.blk.vector
                def _(v, nn=nn, m=m):
                    v.wait_ge(sps.h, nn + 1)
                    if nn >= 2:
                        v.wait_ge(sstq[nn % 2].h, (nn // 2) * 16)
                    sev.inc(v.tensor_copy(ob[0:m, (nn % 2) * 512:(nn % 2) * 512 + 512],
                                          mm[nn % 4][0:m, :]), 1)

                @ctx.blk.sync
                def _(sync, od=od, lr0=lr0, m=m, tbk=tbk, nn=nn, oc=oc):
                    if oc == NOC - 1 and tbk + 2 < NTB:
                        sync.wait_ge(sps.h, (tbk + 1) * NOC)
                        srcx = src_dram.rearrange("(c p) t -> p c t", p=128)[:, :, (tbk + 2) * 512:(tbk + 3) * 512]
                        dstx = xs[:, (tbk % 2) * KC * 512:(tbk % 2 + 1) * KC * 512].rearrange("p (c t) -> p c t", t=512)
                        sx[tbk % 2].inc(sync.dma_start(out=dstx, in_=srcx), 16)
                    sync.wait_ge(sev.h, nn + 1)
                    dst = od[lr0:lr0 + m, tbk * 512:(tbk + 1) * 512]
                    sstq[nn % 2].inc(sync.dma_start(out=dst, in_=ob[0:m, (nn % 2) * 512:(nn % 2) * 512 + 512]), 16)
        return sstq


CH = 256
NCH = T // CH


def make_dtpipe(ctx, tag, dt_src_dram, dtb_dram, alog_dram, do_softplus):
    nc = ctx.nc
    d = {}
    d["dt"] = nc.alloc_sbuf_tensor(tag + "_dt", [32, T], F32)
    d["abt"] = nc.alloc_sbuf_tensor(tag + "_abt", [32, T], F32)
    d["acum"] = nc.alloc_sbuf_tensor(tag + "_acum", [32, CH], F32)
    d["acum_tm"] = nc.alloc_sbuf_tensor(tag + "_acumtm", [128, 2 * 32], F32)
    d["dt_tm"] = nc.alloc_sbuf_tensor(tag + "_dttm", [128, 2 * 32], F32)
    d["dec"] = nc.alloc_sbuf_tensor(tag + "_dec", [32, CH], F32)
    d["dec_tm"] = nc.alloc_sbuf_tensor(tag + "_dectm", [128, 2 * 32], F32)
    d["lamb"] = nc.alloc_sbuf_tensor(tag + "_lamb", [128, 32], F32)
    d["acol"] = nc.alloc_sbuf_tensor(tag + "_acol", [32, 4], F32)
    d["ones"] = nc.alloc_sbuf_tensor(tag + "_onesr", [32, CH], F32)
    d["u"] = nc.alloc_sbuf_tensor(tag + "_u", [32, T], F32)
    d["lrow"] = nc.alloc_sbuf_tensor(tag + "_lrow", [1, 32], F32)
    d["sdt"] = ctx.sem(tag + "_sdt")
    d["sch_c"] = ctx.sem(tag + "_schk")
    vch = ctx.chain(tag + "_dtv")
    sch = ctx.chain(tag + "_dts")
    sld = ctx.sem(tag + "_dtld")
    sdt = d["sdt"]

    @ctx.blk.sync
    def _(sync):
        sld.inc(sync.dma_start(out=d["dt"][:], in_=dt_src_dram), 16)
        sld.inc(sync.dma_start(out=d["acol"][:, 0:1], in_=alog_dram), 16)
        if dtb_dram is not None:
            sld.inc(sync.dma_start(out=d["acol"][:, 1:2], in_=dtb_dram), 16)

    @ctx.blk.scalar
    def _(s):
        sld.wait(s)
        sch.step(s, s.activation(d["acol"][:, 2:3], d["acol"][:, 0:1], AF.Exp))
        sdt.inc(s.nop(), 1)

    @ctx.blk.vector
    def _(v):
        v.wait_ge(sdt.h, 1)
        vch.step(v, v.memset(d["ones"][:], 1.0))
        vch.step(v, v.tensor_scalar(d["acol"][:, 2:3], d["acol"][:, 2:3], -1.0, None, ALU.mult))
        if do_softplus:
            vch.step(v, v.tensor_scalar(d["dt"][:], d["dt"][:], d["acol"][:, 1:2], None, ALU.add))
        sdt.inc(v.nop(), 1)

    if do_softplus:
        @ctx.blk.scalar
        def _(s):
            s.wait_ge(sdt.h, 2)
            sch.step(s, s.activation(d["u"][:], d["dt"][:], AF.Abs))
            sch.step(s, s.activation(d["u"][:], d["u"][:], AF.Exp, scale=-1.0))
            sch.step(s, s.activation(d["u"][:], d["u"][:], AF.Ln, bias=1.0))
            sdt.inc(s.nop(), 1)

        @ctx.blk.vector
        def _(v):
            v.wait_ge(sdt.h, 3)
            vch.step(v, v.tensor_scalar(d["dt"][:], d["dt"][:], 0.0, None, ALU.max))
            vch.step(v, v.tensor_add(d["dt"][:], d["dt"][:], d["u"][:]))
            sdt.inc(v.nop(), 1)
    else:
        @ctx.blk.vector
        def _(v):
            v.wait_ge(sdt.h, 2)
            sdt.inc(v.nop(), 2)

    @ctx.blk.vector
    def _(v):
        v.wait_ge(sdt.h, 4)
        vch.step(v, v.tensor_scalar(d["abt"][:], d["dt"][:], d["acol"][:, 2:3], None, ALU.mult))
        sdt.inc(v.nop(), 1)

    def emit_chunk_dt(c, extra_gate=None):
        vch2 = vch
        sch2 = ctx.sem(tag + "_tr")
        scc = d["sch_c"]

        @ctx.blk.vector
        def _(v, c=c):
            v.wait_ge(sdt.h, 5)
            v.wait_ge(ctx.sem(tag + "_trp").h, ctx.sem(tag + "_trp").n)
            if extra_gate is not None:
                extra_gate(v, c)
            vch2.step(v, v.tensor_tensor_scan(d["acum"][:], d["ones"][:],
                                              d["abt"][:, c * CH:(c + 1) * CH], 0.0,
                                              ALU.mult, ALU.add))
            vch2.step(v, v.tensor_scalar(d["dec"][:], d["acum"][:], d["acum"][:, CH - 1:CH],
                                         None, ALU.subtract))
            scc.inc(v.nop(), 1)

        @ctx.blk.scalar
        def _(s, c=c):
            s.wait_ge(scc.h, 3 * c + 1)
            sch.step(s, s.activation(d["dec"][:], d["dec"][:], AF.Exp, scale=-1.0))
            sch.step(s, s.activation(d["u"][0:32, 0:1], d["acum"][:, CH - 1:CH], AF.Exp))
            scc.inc(s.nop(), 1)

        svt = ctx.sem(tag + "_svt")
        seq = [("dt", d["dt"][0:32, c * CH:(c + 1) * CH], d["dt_tm"]),
               ("dec", d["dec"][0:32, :], d["dec_tm"]),
               ("acum", d["acum"][0:32, :], d["acum_tm"])]
        for (nm, srcT, dstT) in seq:
            for half in range(2):
                @ctx.blk.tensor
                def _(t, srcT=srcT, half=half, c=c):
                    t.wait_ge(scc.h, 3 * c + 2)
                    ctx.sem("const_rdy").wait(t)
                    t.wait_ge(svt.h, svt.n)
                    str32 = t.transpose(ctx.ps["tr32"][0:128, 0:32],
                                        srcT[:, half * 128:(half + 1) * 128],
                                        ctx.sb["ident_f"][0:32, 0:32])
                    ctx.sem(tag + "_trp").inc(str32, 1)

                @ctx.blk.vector
                def _(v, dstT=dstT, half=half):
                    v.wait_ge(ctx.sem(tag + "_trp").h, ctx.sem(tag + "_trp").n)
                    svt.inc(v.tensor_copy(dstT[:, half * 32:(half + 1) * 32],
                                          ctx.ps["tr32"][0:128, 0:32]), 1)

        @ctx.blk.tensor
        def _(t, c=c):
            t.wait_ge(svt.h, svt.n)
            lam_t = t.transpose(ctx.ps["tr32"][0:1, 0:32], d["u"][0:32, 0:1], ctx.sb["ident_f"][0:32, 0:32])
            ctx.sem(tag + "_trp").inc(lam_t, 1)

        @ctx.blk.vector
        def _(v):
            v.wait_ge(ctx.sem(tag + "_trp").h, ctx.sem(tag + "_trp").n)
            svt.inc(v.tensor_copy(d["lrow"][:], ctx.ps["tr32"][0:1, 0:32]), 1)

        @ctx.blk.tensor
        def _(t, c=c):
            t.wait_ge(svt.h, svt.n)
            ins = t.matmul(ctx.ps["mm"][3][:, 0:32], ctx.sb["ones_row_f"][:], d["lrow"][:],
                           start=True, stop=True)
            ctx.sem(tag + "_trp").inc(ins, 1)

        @ctx.blk.vector
        def _(v, c=c):
            v.wait_ge(ctx.sem(tag + "_trp").h, ctx.sem(tag + "_trp").n)
            vch2.step(v, v.tensor_copy(d["lamb"][:], ctx.ps["mm"][3][:, 0:32]))
            scc.inc(v.nop(), 1)

    d["emit_chunk_dt"] = emit_chunk_dt
    return d


def emit_l1a(ctx, tag, hbc_dram, dt_pre_dram, tail_dram, flag_dram, convw_dram,
             convb_dram, dtb_dram, alog_dram, stash_dram, dtT_dram, cst_dram, s8_dram):
    nc = ctx.nc
    CD = 6144
    NCT = CD // 128
    mm = ctx.ps["mm"]
    trb = ctx.ps["trb"]
    crdy = ctx.sem("const_rdy")

    hb = nc.alloc_sbuf_tensor(tag + "_hb", [128, 2 * NCT * 259], BF16)
    co = nc.alloc_sbuf_tensor(tag + "_co", [128, NCT * CH], BF16)
    acc = nc.alloc_sbuf_tensor(tag + "_acc", [128, 2 * CH], BF16)
    cw = nc.alloc_sbuf_tensor(tag + "_cw", [128, NCT * 4], F32)
    cb = nc.alloc_sbuf_tensor(tag + "_cb", [128, NCT], F32)
    flagc = nc.alloc_sbuf_tensor(tag + "_flag", [128, 1], F32)
    xtm = nc.alloc_sbuf_tensor(tag + "_xtm", [128, 256], BF16)
    xdd = nc.alloc_sbuf_tensor(tag + "_xdd", [128, 256], BF16)
    ddc = nc.alloc_sbuf_tensor(tag + "_ddc", [128, 2], F32)
    btm = nc.alloc_sbuf_tensor(tag + "_btm", [128, 256], BF16)
    css = nc.alloc_sbuf_tensor(tag + "_cs", [128, 2 * 128], F32)
    S = nc.alloc_sbuf_tensor(tag + "_S", [128, 32 * 128], F32)

    sld = [ctx.sem(tag + f"_ld{j}") for j in range(2)]
    scin = ctx.sem(tag + "_cin")
    sconv = ctx.sem(tag + "_conv")
    sstash = [ctx.sem(tag + f"_sst{j}") for j in range(2)]
    strx = ctx.sem(tag + "_trx")
    sevx = ctx.sem(tag + "_evx")
    sxd = ctx.sem(tag + "_xd")
    sbtm = ctx.sem(tag + "_btm")
    sms = ctx.sem(tag + "_ms")
    sevs = ctx.sem(tag + "_evs")
    sstc = [ctx.sem(tag + f"_stc{j}") for j in range(2)]
    sdts = ctx.sem(tag + "_dts")
    shd = ctx.sem(tag + "_shd")
    sS0 = ctx.sem(tag + "_s0")
    vch = ctx.chain(tag + "_v")
    sch = ctx.chain(tag + "_s")

    dtp = make_dtpipe(ctx, tag + "dt", dt_pre_dram, dtb_dram, alog_dram, True)

    @ctx.blk.sync
    def _(sync):
        scin.inc(sync.dma_start(out=cw[:], in_=convw_dram), 16)
        scin.inc(sync.dma_start(out=cb[:], in_=convb_dram), 16)
        scin.inc(sync.dma_start(out=flagc[:], in_=flag_dram), 16)

    @ctx.blk.vector
    def _(v):
        v.wait_ge(dtp["sdt"].h, 5)
        sS0.inc(v.memset(S[:], 0.0), 1)

    @ctx.blk.sync
    def _(sync):
        sync.wait_ge(dtp["sdt"].h, 5)
        sdts.inc(sync.dma_start(out=dtT_dram, in_=dtp["dt"][:]), 16)

    def issue_slab(sync, c):
        j = c % 2
        base = j * NCT * 259
        dst = hb[:, base:base + NCT * 259].rearrange("p (i e) -> p i e", e=259)
        sld[j].inc(sync.dma_start(out=dst[:, :, 3:259],
                                  in_=hbc_dram.rearrange("(i p) t -> p i t", p=128)[:, :, c * CH:(c + 1) * CH]), 16)
        if c == 0:
            sld[j].inc(sync.dma_start(out=dst[:, :, 0:3],
                                      in_=tail_dram.rearrange("(i p) e -> p i e", p=128)), 16)
        else:
            sld[j].inc(sync.dma_start(out=dst[:, :, 0:3],
                                      in_=hbc_dram.rearrange("(i p) t -> p i t", p=128)[:, :, c * CH - 3:c * CH]), 16)

    @ctx.blk.sync
    def _(sync):
        issue_slab(sync, 0)
        issue_slab(sync, 1)

    for c in range(NCH):
        j = c % 2
        base = j * NCT * 259

        @ctx.blk.vector
        def _(v, c=c, base=base):
            v.wait_ge(sld[c % 2].h, (c // 2 + 1) * 32)
            scin.wait(v)
            if c == 0:
                vch.step(v, v.tensor_scalar(
                    hb[:, base:base + NCT * 259].rearrange("p (i e) -> p i e", e=259)[:, :, 0:3],
                    hb[:, base:base + NCT * 259].rearrange("p (i e) -> p i e", e=259)[:, :, 0:3],
                    flagc[:, 0:1], None, ALU.mult))
            ctx.sem(tag + "_slabrdy").inc(v.nop(), 1)

        for i in range(NCT):
            @ctx.blk.vector
            def _(v, c=c, i=i, base=base):
                v.wait_ge(ctx.sem(tag + "_slabrdy").h, c + 1)
                if c > 0 or i > 0:
                    n = c * NCT + i
                    if n >= 2:
                        v.wait_ge(sconv.h, n - 1)
                w0 = hb[:, base + i * 259 + 0: base + i * 259 + 256]
                w1 = hb[:, base + i * 259 + 1: base + i * 259 + 257]
                w2 = hb[:, base + i * 259 + 2: base + i * 259 + 258]
                w3 = hb[:, base + i * 259 + 3: base + i * 259 + 259]
                a0 = acc[:, (i % 2) * CH:(i % 2) * CH + CH]
                vch.step(v, v.tensor_scalar(a0, w0, cw[:, i * 4 + 0:i * 4 + 1], None, ALU.mult))
                vch.step(v, v.scalar_tensor_tensor(a0, w1, cw[:, i * 4 + 1:i * 4 + 2], a0, ALU.mult, ALU.add))
                vch.step(v, v.scalar_tensor_tensor(a0, w2, cw[:, i * 4 + 2:i * 4 + 3], a0, ALU.mult, ALU.add))
                vch.step(v, v.scalar_tensor_tensor(a0, w3, cw[:, i * 4 + 3:i * 4 + 4], a0, ALU.mult, ALU.add))
                ctx.sem(tag + "_cacc").inc(v.nop(), 1)

            @ctx.blk.scalar
            def _(s, c=c, i=i):
                s.wait_ge(ctx.sem(tag + "_cacc").h, c * NCT + i + 1)
                if c > 0:
                    s.wait_ge(sstash[(c - 1) % 2].h, ((c - 1) // 2 + 1) * 16)
                    s.wait_ge(shd.h, c * 32)
                sch.step(s, s.activation(co[:, i * CH:(i + 1) * CH], acc[:, (i % 2) * CH:(i % 2) * CH + CH],
                                         AF.Silu, bias=cb[:, i:i + 1]))
                sconv.inc(s.nop(), 1)

        @ctx.blk.sync
        def _(sync, c=c, j=j):
            sync.wait_ge(sconv.h, (c + 1) * NCT)
            dst = stash_dram.rearrange("(i p) t -> p i t", p=128)[:, :, c * CH:(c + 1) * CH]
            sstash[j].inc(sync.dma_start(out=dst, in_=co[:].rearrange("p (i t) -> p i t", t=CH)), 16)
            if c + 2 < NCH:
                issue_slab(sync, c + 2)

        dtp["emit_chunk_dt"](c, extra_gate=(lambda v, cc=c: v.wait_ge(shd.h, cc * 32) if cc > 0 else None))

        for hi in range(32):
            g = hi // 4
            h = g + 8 * (hi % 4)
            nh = c * 32 + hi

            @ctx.blk.tensor
            def _(t, c=c, h=h, g=g, nh=nh, hi=hi):
                t.wait_ge(sconv.h, (c + 1) * NCT)
                crdy.wait(t)
                if nh >= 2:
                    t.wait_ge(sevx.h, 2 * (nh - 1))
                for hh in range(2):
                    ins = t.transpose(trb[nh % 2][:, hh * 128:(hh + 1) * 128],
                                      co[:, h * CH + hh * 128: h * CH + (hh + 1) * 128],
                                      ctx.sb["ident_bf"][:])
                if hi % 4 == 0:
                    for hh in range(2):
                        ins = t.transpose(trb[nh % 2][:, 256 + hh * 128: 256 + (hh + 1) * 128],
                                          co[:, (32 + g) * CH + hh * 128: (32 + g) * CH + (hh + 1) * 128],
                                          ctx.sb["ident_bf"][:])
                strx.inc(ins, 1)

            @ctx.blk.vector
            def _(v, c=c, h=h, g=g, nh=nh, hi=hi):
                v.wait_ge(strx.h, nh + 1)
                v.wait_ge(dtp["sch_c"].h, 3 * (c + 1))
                if nh >= 1:
                    v.wait_ge(sms.h, nh)
                vch.step(v, v.tensor_copy(xtm[:], trb[nh % 2][:, 0:256]))
                if hi % 4 == 0:
                    vch.step(v, v.tensor_copy(btm[:], trb[nh % 2][:, 256:512]))
                sevx.inc(v.nop(), 2)
                for hh in range(2):
                    vch.step(v, v.tensor_mul(ddc[:, hh:hh + 1],
                                             dtp["dt_tm"][:, hh * 32 + h: hh * 32 + h + 1],
                                             dtp["dec_tm"][:, hh * 32 + h: hh * 32 + h + 1]))
                    vch.step(v, v.tensor_scalar(xdd[:, hh * 128:(hh + 1) * 128],
                                                xtm[:, hh * 128:(hh + 1) * 128],
                                                ddc[:, hh:hh + 1], None, ALU.mult))
                sxd.inc(v.nop(), 1)

            @ctx.blk.tensor
            def _(t, c=c, h=h, nh=nh):
                t.wait_ge(sxd.h, nh + 1)
                if nh >= 3:
                    t.wait_ge(sevs.h, nh - 2)
                for hh in range(2):
                    ins = t.matmul(mm[nh % 3][:, 0:128],
                                   btm[:, hh * 128:(hh + 1) * 128],
                                   xdd[:, hh * 128:(hh + 1) * 128],
                                   start=(hh == 0), stop=(hh == 1))
                sms.inc(ins, 1)

            @ctx.blk.vector
            def _(v, c=c, h=h, nh=nh):
                v.wait_ge(sms.h, nh + 1)
                if nh == 0:
                    v.wait_ge(sS0.h, 1)
                if nh >= 2:
                    v.wait_ge(sstc[nh % 2].h, (nh // 2) * 16)
                vch.step(v, v.tensor_copy(css[:, (nh % 2) * 128:(nh % 2) * 128 + 128], mm[nh % 3][:, 0:128]))
                vch.step(v, v.scalar_tensor_tensor(S[:, h * 128:(h + 1) * 128],
                                                   S[:, h * 128:(h + 1) * 128],
                                                   dtp["lamb"][:, h:h + 1],
                                                   css[:, (nh % 2) * 128:(nh % 2) * 128 + 128],
                                                   ALU.mult, ALU.add))
                sevs.inc(v.nop(), 1)
                shd.inc(v.nop(), 1)

            @ctx.blk.sync
            def _(sync, c=c, h=h, nh=nh):
                sync.wait_ge(sevs.h, nh + 1)
                sstc[nh % 2].inc(sync.dma_start(
                    out=cst_dram[c * 128:(c + 1) * 128, h * 128:(h + 1) * 128],
                    in_=css[:, (nh % 2) * 128:(nh % 2) * 128 + 128]), 16)

    sS8 = ctx.sem(tag + "_s8")

    @ctx.blk.sync
    def _(sync):
        sync.wait_ge(sevs.h, NCH * 32)
        sS8.inc(sync.dma_start(out=s8_dram, in_=S[:]), 16)
    return [sS8, sdts] + sstash + sstc


def emit_l1b(ctx, tag, stash_dram, dtT_dram, gate_dram, cst_dram, s8in_dram,
             flag_dram, nw_dram, dD_dram, alog_dram, norm1_dram):
    nc = ctx.nc
    CD = 6144
    NCT = CD // 128
    mm = ctx.ps["mm"]
    trb = ctx.ps["trb"]
    prow = ctx.ps["row"]
    crdy = ctx.sem("const_rdy")
    GEPS = 1e-6

    selm = np.zeros((32, 32 * 128), dtype=np.float32)
    for h in range(32):
        selm[h, h * 128:(h + 1) * 128] = 1.0
    ctx.dram[tag + "_sel"] = nc.inline_tensor(selm, name=tag + "_c_sel").ap()
    tri = np.zeros((2, 128, 256), dtype=np.float32)
    for half in range(2):
        for s_ in range(128):
            tri[half, s_, half * 128 + s_:] = 1.0
    ctx.dram[tag + "_tri0"] = nc.inline_tensor(tri[0], name=tag + "_c_tri0").ap()
    ctx.dram[tag + "_tri1"] = nc.inline_tensor(tri[1], name=tag + "_c_tri1").ap()

    sel = nc.alloc_sbuf_tensor(tag + "_sel", [32, 32 * 128], F32)
    tri0 = nc.alloc_sbuf_tensor(tag + "_tri0", [128, 256], F32)
    tri1 = nc.alloc_sbuf_tensor(tag + "_tri1", [128, 256], F32)
    co = nc.alloc_sbuf_tensor(tag + "_co", [128, 2 * NCT * CH], BF16)
    gt = nc.alloc_sbuf_tensor(tag + "_gt", [128, 2 * 32 * CH], BF16)
    S = nc.alloc_sbuf_tensor(tag + "_S", [128, 32 * 128], F32)
    sbf = nc.alloc_sbuf_tensor(tag + "_sbf", [128, 128], BF16)
    cprev = nc.alloc_sbuf_tensor(tag + "_cprev", [128, 2 * 32 * 128], F32)
    lamall = nc.alloc_sbuf_tensor(tag + "_lamall", [128, 8 * 32], F32)
    rsb = nc.alloc_sbuf_tensor(tag + "_rsb", [128, 256], F32)
    ersb = nc.alloc_sbuf_tensor(tag + "_ersb", [128, 256], BF16)
    csb = nc.alloc_sbuf_tensor(tag + "_csb", [128, 256], BF16)
    msd = nc.alloc_sbuf_tensor(tag + "_msd", [128, 256], F32)
    mseL = nc.alloc_sbuf_tensor(tag + "_msel", [128, 256], F32)
    msb = nc.alloc_sbuf_tensor(tag + "_msb", [128, 2 * 256], BF16)
    xdt = nc.alloc_sbuf_tensor(tag + "_xdt", [128, 256], BF16)
    ygrp = nc.alloc_sbuf_tensor(tag + "_ygrp", [128, 4 * 256], F32)
    gsil = nc.alloc_sbuf_tensor(tag + "_gsil", [128, 256], BF16)
    sqg = nc.alloc_sbuf_tensor(tag + "_sqg", [128, 256], BF16)
    rrow = nc.alloc_sbuf_tensor(tag + "_rrow", [1, 256], F32)
    rstb = nc.alloc_sbuf_tensor(tag + "_rstb", [128, 256], F32)
    nwc = nc.alloc_sbuf_tensor(tag + "_nw", [128, 32], F32)
    Db = nc.alloc_sbuf_tensor(tag + "_Db", [128, 32], F32)
    dcol = nc.alloc_sbuf_tensor(tag + "_dcol", [32, 1], F32)
    drow = nc.alloc_sbuf_tensor(tag + "_drow", [1, 32], F32)
    flagc = nc.alloc_sbuf_tensor(tag + "_flag", [128, 1], F32)
    nslab = nc.alloc_sbuf_tensor(tag + "_nsl", [128, 32 * CH], BF16)

    scin = ctx.sem(tag + "_cin")
    sS0 = ctx.sem(tag + "_s0")
    sld = [ctx.sem(tag + f"_ld{j}") for j in range(2)]
    scp = [ctx.sem(tag + f"_cp{j}") for j in range(2)]
    sRh = ctx.sem(tag + "_Rh")
    sRe = ctx.sem(tag + "_Re")
    sG = ctx.sem(tag + "_G")
    sMp = ctx.sem(tag + "_Mp")
    sXp = ctx.sem(tag + "_Xp")
    sXe = ctx.sem(tag + "_Xe")
    sY = ctx.sem(tag + "_Y")
    sYe = ctx.sem(tag + "_Ye")
    sQn = ctx.sem(tag + "_Qn")
    sRs = ctx.sem(tag + "_Rs")
    sNo = ctx.sem(tag + "_No")
    sSb = ctx.sem(tag + "_Sb")
    sup = ctx.sem(tag + "_up")
    snst = [ctx.sem(tag + f"_nst{j}") for j in range(2)]
    vch = ctx.chain(tag + "_v")
    sch = ctx.chain(tag + "_s")

    dtp = make_dtpipe(ctx, tag + "dt", dtT_dram, None, alog_dram, False)

    @ctx.blk.sync
    def _(sync):
        scin.inc(sync.dma_start(out=sel[:], in_=ctx.dram[tag + "_sel"]), 16)
        scin.inc(sync.dma_start(out=tri0[:], in_=ctx.dram[tag + "_tri0"]), 16)
        scin.inc(sync.dma_start(out=tri1[:], in_=ctx.dram[tag + "_tri1"]), 16)
        scin.inc(sync.dma_start(out=nwc[:], in_=nw_dram), 16)
        scin.inc(sync.dma_start(out=dcol[:], in_=dD_dram), 16)
        scin.inc(sync.dma_start(out=flagc[:], in_=flag_dram), 16)
        scin.inc(sync.dma_start(out=S[:], in_=s8in_dram), 16)

    @ctx.blk.tensor
    def _(t):
        scin.wait(t)
        crdy.wait(t)
        ctx.sem(tag + "_dtr").inc(
            t.transpose(ctx.ps["tr32"][0:1, 0:32], dcol[:], ctx.sb["ident_f"][0:32, 0:32]), 1)

    @ctx.blk.vector
    def _(v):
        v.wait_ge(ctx.sem(tag + "_dtr").h, 1)
        vch.step(v, v.tensor_copy(drow[:], ctx.ps["tr32"][0:1, 0:32]))
        vch.step(v, v.tensor_scalar(S[:], S[:], flagc[:, 0:1], None, ALU.mult))
        sS0.inc(v.nop(), 1)

    @ctx.blk.tensor
    def _(t):
        t.wait_ge(sS0.h, 1)
        ctx.sem(tag + "_dtr").inc(
            t.matmul(mm[2][:, 0:32], ctx.sb["ones_row_f"][:], drow[:], start=True, stop=True), 1)

    @ctx.blk.vector
    def _(v):
        v.wait_ge(ctx.sem(tag + "_dtr").h, 2)
        vch.step(v, v.tensor_copy(Db[:], mm[2][:, 0:32]))
        sS0.inc(v.nop(), 1)

    def issue_slab(sync, c):
        j = c % 2
        sld[j].inc(sync.dma_start(
            out=co[:, j * NCT * CH:(j + 1) * NCT * CH].rearrange("p (i t) -> p i t", t=CH),
            in_=stash_dram.rearrange("(i p) t -> p i t", p=128)[:, :, c * CH:(c + 1) * CH]), 16)
        sld[j].inc(sync.dma_start(
            out=gt[:, j * 32 * CH:(j + 1) * 32 * CH].rearrange("p (i t) -> p i t", t=CH),
            in_=gate_dram.rearrange("(i p) t -> p i t", p=128)[:, :, c * CH:(c + 1) * CH]), 16)
        if c > 0:
            sld[j].inc(sync.dma_start(
                out=cprev[:, j * 32 * 128:(j + 1) * 32 * 128],
                in_=cst_dram[(c - 1) * 128:c * 128, :]), 16)

    @ctx.blk.sync
    def _(sync):
        issue_slab(sync, 0)
        issue_slab(sync, 1)

    for c in range(NCH):
        j = c % 2
        cob = j * NCT * CH
        gtb = j * 32 * CH
        ldcnt = {0: sld[0].n, 1: sld[1].n}

        dtp["emit_chunk_dt"](c, extra_gate=(lambda v, cc=c: v.wait_ge(sMp.h, cc * 32) if cc > 0 else None))

        @ctx.blk.vector
        def _(v, c=c):
            v.wait_ge(dtp["sch_c"].h, 3 * (c + 1))
            vch.step(v, v.tensor_copy(lamall[:, c * 32:(c + 1) * 32], dtp["lamb"][:]))
            ctx.sem(tag + "_lam").inc(v.nop(), 1)

        if c > 0:
            @ctx.blk.vector
            def _(v, c=c, j=j):
                v.wait_ge(sld[j].h, ldcnt[j])
                v.wait_ge(ctx.sem(tag + "_lam").h, c)
                v.wait_ge(sSb.h, c * 32)
                for h in range(32):
                    ins = v.scalar_tensor_tensor(S[:, h * 128:(h + 1) * 128],
                                                 S[:, h * 128:(h + 1) * 128],
                                                 lamall[:, (c - 1) * 32 + h:(c - 1) * 32 + h + 1],
                                                 cprev[:, j * 32 * 128 + h * 128: j * 32 * 128 + (h + 1) * 128],
                                                 ALU.mult, ALU.add)
                vch.sem.inc(ins, 1)
                v.wait_ge(vch.sem.h, vch.sem.n)
                sup.inc(v.nop(), 1)

        for h in range(32):
            g = h % 8
            nh = c * 32 + h

            @ctx.blk.tensor
            def _(t, c=c, h=h, nh=nh):
                if nh == 0:
                    t.wait_ge(sS0.h, 2)
                t.wait_ge(dtp["sch_c"].h, 3 * (c + 1))
                t.wait_ge(sRe.h, nh)
                if nh >= 4:
                    t.wait_ge(ctx.sem(tag + "_rsc").h, nh // 4)
                sRh.inc(t.matmul(mm[2][:, 0:256], sel[:, h * 128:(h + 1) * 128],
                                 dtp["acum"][:], start=True, stop=True), 1)

            @ctx.blk.vector
            def _(v, nh=nh):
                v.wait_ge(sRh.h, nh + 1)
                sRe.inc(v.tensor_copy(rsb[:], mm[2][:, 0:256]), 1)

            @ctx.blk.scalar
            def _(s, nh=nh, c=c):
                s.wait_ge(sRe.h, nh + 1)
                sch.step(s, s.activation(ersb[:], rsb[:], AF.Exp))
                ctx.sem(tag + "_er").inc(s.nop(), 1)

            @ctx.blk.vector
            def _(v, c=c, h=h, g=g, nh=nh, cob=cob, j=j):
                v.wait_ge(ctx.sem(tag + "_er").h, nh + 1)
                v.wait_ge(sld[j].h, ldcnt[j])
                if nh >= 1:
                    v.wait_ge(sY.h, nh)
                vch.step(v, v.tensor_mul(csb[:], co[:, cob + (40 + g) * CH: cob + (40 + g) * CH + CH], ersb[:]))
                ctx.sem(tag + "_cs").inc(v.nop(), 1)

            @ctx.blk.tensor
            def _(t, c=c, h=h, g=g, nh=nh, cob=cob):
                t.wait_ge(ctx.sem(tag + "_cs").h, nh + 1)
                if nh >= 1:
                    t.wait_ge(sMp.h, nh)
                for hh in range(2):
                    ins = t.matmul(mm[0][:, hh * 256:(hh + 1) * 256],
                                   co[:, cob + (32 + g) * CH + hh * 128: cob + (32 + g) * CH + (hh + 1) * 128],
                                   co[:, cob + (40 + g) * CH: cob + (40 + g) * CH + CH],
                                   start=(hh == 0), stop=(hh == 1))
                sG.inc(ins, 1)

            @ctx.blk.vector
            def _(v, c=c, h=h, nh=nh):
                v.wait_ge(sG.h, nh + 1)
                if nh >= 1:
                    v.wait_ge(sY.h, nh)
                for hh in range(2):
                    vch.step(v, v.tensor_scalar(msd[:], rsb[:],
                                                dtp["acum_tm"][:, hh * 32 + h: hh * 32 + h + 1],
                                                0.0, ALU.subtract, ALU.min))
                    ctx.sem(tag + "_md").inc(v.nop(), 1)
                    v.wait_ge(ctx.sem(tag + "_me").h, 2 * nh + hh + 1)
                    tri = tri0 if hh == 0 else tri1
                    vch.step(v, v.tensor_mul(mseL[:], mseL[:], tri[:]))
                    vch.step(v, v.tensor_tensor(msb[:, hh * 256:(hh + 1) * 256], mseL[:],
                                                mm[0][:, hh * 256:(hh + 1) * 256], ALU.mult))
                sMp.inc(v.nop(), 1)

            @ctx.blk.scalar
            def _(s, nh=nh):
                for hh in range(2):
                    s.wait_ge(ctx.sem(tag + "_md").h, 2 * nh + hh + 1)
                    sch.step(s, s.activation(mseL[:], msd[:], AF.Exp))
                    ctx.sem(tag + "_me").inc(s.nop(), 1)

            @ctx.blk.tensor
            def _(t, c=c, h=h, nh=nh, cob=cob):
                t.wait_ge(sld[c % 2].h, ldcnt[c % 2])
                crdy.wait(t)
                if nh >= 2:
                    t.wait_ge(sXe.h, nh - 1)
                for hh in range(2):
                    ins = t.transpose(trb[nh % 2][:, hh * 128:(hh + 1) * 128],
                                      co[:, cob + h * CH + hh * 128: cob + h * CH + (hh + 1) * 128],
                                      ctx.sb["ident_bf"][:])
                sXp.inc(ins, 1)

            @ctx.blk.vector
            def _(v, h=h, nh=nh):
                v.wait_ge(sXp.h, nh + 1)
                if nh >= 1:
                    v.wait_ge(sY.h, nh)
                for hh in range(2):
                    vch.step(v, v.tensor_scalar(xdt[:, hh * 128:(hh + 1) * 128],
                                                trb[nh % 2][:, hh * 128:(hh + 1) * 128],
                                                dtp["dt_tm"][:, hh * 32 + h: hh * 32 + h + 1],
                                                None, ALU.mult))
                sXe.inc(v.nop(), 1)

            @ctx.blk.vector
            def _(v, c=c, h=h, nh=nh):
                if c > 0:
                    v.wait_ge(sup.h, c)
                if nh >= 1:
                    v.wait_ge(sY.h, nh)
                vch.step(v, v.tensor_copy(sbf[:], S[:, h * 128:(h + 1) * 128]))
                sSb.inc(v.nop(), 1)

            @ctx.blk.tensor
            def _(t, nh=nh):
                t.wait_ge(sMp.h, nh + 1)
                t.wait_ge(sXe.h, nh + 1)
                t.wait_ge(sSb.h, nh + 1)
                t.wait_ge(ctx.sem(tag + "_cs").h, nh + 1)
                if nh >= 1:
                    t.wait_ge(sYe.h, nh)
                for hh in range(2):
                    t.matmul(mm[1][:, 0:256], xdt[:, hh * 128:(hh + 1) * 128],
                             msb[:, hh * 256:(hh + 1) * 256], start=(hh == 0), stop=False)
                ins = t.matmul(mm[1][:, 0:256], sbf[:], csb[:], start=False, stop=True)
                sY.inc(ins, 1)

            @ctx.blk.vector
            def _(v, c=c, h=h, nh=nh, cob=cob):
                v.wait_ge(sY.h, nh + 1)
                v.wait_ge(ctx.sem(tag + "_gs").h, nh + 1)
                if nh >= 4:
                    v.wait_ge(sNo.h, (nh // 4) * 4)
                vch.step(v, v.scalar_tensor_tensor(ygrp[:, (h % 4) * 256:(h % 4) * 256 + 256],
                                                   co[:, cob + h * CH: cob + (h + 1) * CH],
                                                   Db[:, h:h + 1], mm[1][:, 0:256],
                                                   ALU.mult, ALU.add))
                vch.step(v, v.tensor_mul(ygrp[:, (h % 4) * 256:(h % 4) * 256 + 256],
                                         ygrp[:, (h % 4) * 256:(h % 4) * 256 + 256], gsil[:]))
                sYe.inc(v.nop(), 1)

            @ctx.blk.scalar
            def _(s, c=c, h=h, nh=nh, gtb=gtb):
                s.wait_ge(sld[c % 2].h, ldcnt[c % 2])
                if nh >= 1:
                    s.wait_ge(sYe.h, nh)
                sch.step(s, s.activation(gsil[:], gt[:, gtb + h * CH: gtb + (h + 1) * CH], AF.Silu))
                ctx.sem(tag + "_gs").inc(s.nop(), 1)
                s.wait_ge(sYe.h, nh + 1)
                if nh >= 1:
                    s.wait_ge(sQn.h, nh)
                sch.step(s, s.activation(sqg[:], ygrp[:, (h % 4) * 256:(h % 4) * 256 + 256], AF.Square))
                ctx.sem(tag + "_sq").inc(s.nop(), 1)

            @ctx.blk.tensor
            def _(t, h=h, nh=nh):
                t.wait_ge(ctx.sem(tag + "_sq").h, nh + 1)
                if h % 4 == 0:
                    t.wait_ge(ctx.sem(tag + "_rse").h, nh // 4 if nh >= 4 else 0)
                ins = t.matmul(prow[:, 0:256], ctx.sb["ones_col_bf"][:], sqg[:],
                               start=(h % 4 == 0), stop=(h % 4 == 3))
                sQn.inc(ins, 1)

            if h % 4 == 3:
                ng = nh // 4

                @ctx.blk.vector
                def _(v, ng=ng):
                    v.wait_ge(sQn.h, (ng + 1) * 4)
                    vch.step(v, v.tensor_copy(rrow[:], prow[:, 0:256]))
                    ctx.sem(tag + "_rse").inc(v.nop(), 1)

                @ctx.blk.scalar
                def _(s, ng=ng):
                    s.wait_ge(ctx.sem(tag + "_rse").h, ng + 1)
                    sch.step(s, s.activation(rrow[:], rrow[:], AF.Ln,
                                             bias=ctx.sb["misc"][0:1, 2:3], scale=1.0 / 512))
                    sch.step(s, s.activation(rrow[:], rrow[:], AF.Exp, scale=-0.5))
                    ctx.sem(tag + "_rs2").inc(s.nop(), 1)

                @ctx.blk.tensor
                def _(t, ng=ng, nh=nh):
                    t.wait_ge(ctx.sem(tag + "_rs2").h, ng + 1)
                    t.wait_ge(sRe.h, nh + 1)
                    ins = t.matmul(mm[2][:, 0:256], ctx.sb["ones_row_f"][:], rrow[:],
                                   start=True, stop=True)
                    ctx.sem(tag + "_rsb").inc(ins, 1)

                @ctx.blk.vector
                def _(v, ng=ng, c=c, go=h // 4):
                    v.wait_ge(ctx.sem(tag + "_rsb").h, ng + 1)
                    if c > 0:
                        v.wait_ge(snst[(c - 1) % 2].h, ((c - 1) // 2 + 1) * 16)
                    vch.step(v, v.tensor_copy(rstb[:], mm[2][:, 0:256]))
                    ctx.sem(tag + "_rsc").inc(v.nop(), 1)
                    for h2 in range(4 * go, 4 * go + 4):
                        vch.step(v, v.tensor_mul(msd[:], ygrp[:, (h2 % 4) * 256:(h2 % 4) * 256 + 256], rstb[:]))
                        vch.step(v, v.tensor_scalar(nslab[:, h2 * CH:(h2 + 1) * CH], msd[:],
                                                    nwc[:, h2:h2 + 1], None, ALU.mult))
                    sNo.inc(v.nop(), 4)

        @ctx.blk.sync
        def _(sync, c=c):
            sync.wait_ge(sNo.h, (c + 1) * 32)
            dst = norm1_dram.rearrange("(i p) t -> p i t", p=128)[:, :, c * CH:(c + 1) * CH]
            snst[c % 2].inc(sync.dma_start(out=dst, in_=nslab[:].rearrange("p (i t) -> p i t", t=CH)), 16)
            if c + 2 < NCH:
                issue_slab(sync, c + 2)
    return snst


def emit_moe(ctx, tag, xn_dram, xn32_dram, gwT_d, glaT_d, glbT_d, gAT_d, gBT_d, dAT_d, dBT_d,
             gup_bf_d, down_bf_d, f1_dram):
    nc = ctx.nc
    mm = ctx.ps["mm"]
    crdy = ctx.sem("const_rdy")
    SC = 2.0

    gwT = nc.alloc_sbuf_tensor(tag + "_gwT", [128, 16 * 8], F32)
    glaT = nc.alloc_sbuf_tensor(tag + "_glaT", [128, 16 * 16], F32)
    glbT = nc.alloc_sbuf_tensor(tag + "_glbT", [16, 8], F32)
    tg32 = nc.alloc_sbuf_tensor(tag + "_tg32", [16, 512], F32)
    xns32 = nc.alloc_sbuf_tensor(tag + "_xns32", [128, 2 * 512], F32)
    gAT = nc.alloc_sbuf_tensor(tag + "_gAT", [128, 16 * 128], BF16)
    big16 = nc.alloc_sbuf_tensor(tag + "_big16", [48, 20480], BF16)
    gBT = big16[0:16, 0:16384]
    dBT = big16[32:48, 0:16384]
    tg = big16[0:16, 16384:16896]
    tdn = big16[32:48, 16384:20480]
    dAT = nc.alloc_sbuf_tensor(tag + "_dAT", [128, 8 * 128], BF16)
    xns = nc.alloc_sbuf_tensor(tag + "_xns", [128, 16 * 512], BF16)
    gws = nc.alloc_sbuf_tensor(tag + "_gws", [128, 2 * 16 * 256], BF16)
    dws = nc.alloc_sbuf_tensor(tag + "_dws", [128, 2 * 64 * 128], BF16)
    lgf = nc.alloc_sbuf_tensor(tag + "_lgf", [8, 512], F32)
    lgtm = nc.alloc_sbuf_tensor(tag + "_lgtm", [128, 4 * 8], F32)
    prb = nc.alloc_sbuf_tensor(tag + "_prb", [128, 4 * 8], F32)
    msk = nc.alloc_sbuf_tensor(tag + "_msk", [128, 8], F32)
    cwt = nc.alloc_sbuf_tensor(tag + "_cwt", [128, 4 * 8], F32)
    ccol = nc.alloc_sbuf_tensor(tag + "_ccol", [128, 6], F32)
    crow = nc.alloc_sbuf_tensor(tag + "_crow", [1, 512], F32)
    cbst = nc.alloc_sbuf_tensor(tag + "_cbst", [128, 8 * 512], BF16)
    gsil = nc.alloc_sbuf_tensor(tag + "_gsil", [128, 8 * 512], BF16)
    hstk = nc.alloc_sbuf_tensor(tag + "_hstk", [128, 8 * 8 * 512], BF16)
    ob = nc.alloc_sbuf_tensor(tag + "_ob", [128, 2 * 512], BF16)

    scst = ctx.sem(tag + "_cst")
    sxl = [ctx.sem(tag + f"_xl{j}") for j in range(2)]
    sgw = [ctx.sem(tag + f"_gw{j}") for j in range(2)]
    sdw = [ctx.sem(tag + f"_dw{j}") for j in range(2)]
    srt = ctx.sem(tag + "_rt")
    str2 = ctx.sem(tag + "_tr")
    sev2 = ctx.sem(tag + "_ev")
    sgup = ctx.sem(tag + "_gup")
    sgev = ctx.sem(tag + "_gev")
    ster = ctx.sem(tag + "_ter")
    stev = ctx.sem(tag + "_tev")
    sdn = ctx.sem(tag + "_dn")
    sdev = ctx.sem(tag + "_dev")
    sstf = [ctx.sem(tag + f"_st{j}") for j in range(2)]
    vch = ctx.chain(tag + "_v")
    sch = ctx.chain(tag + "_s")

    @ctx.blk.gpsimd
    def _(g):
        scst.inc(g.dma_start(out=gwT[:].rearrange("p (k o) -> p k o", o=8),
                             in_=gwT_d.rearrange("(k p) o -> p k o", p=128)), 16)
        scst.inc(g.dma_start(out=glaT[:].rearrange("p (k o) -> p k o", o=16),
                             in_=glaT_d.rearrange("(k p) o -> p k o", p=128)), 16)
        scst.inc(g.dma_start(out=glbT[:], in_=glbT_d), 16)
        scst.inc(g.dma_start(out=gAT[:].rearrange("p (k o) -> p k o", o=128),
                             in_=gAT_d.rearrange("(k p) o -> p k o", p=128)), 16)
        scst.inc(g.dma_start(out=gBT, in_=gBT_d), 16)
        scst.inc(g.dma_start(out=dAT[:].rearrange("p (k o) -> p k o", o=128),
                             in_=dAT_d.rearrange("(k p) o -> p k o", p=128)), 16)
        scst.inc(g.dma_start(out=dBT, in_=dBT_d), 16)

    @ctx.blk.sync
    def _(sync):
        srcx = xn_dram.rearrange("(c p) t -> p c t", p=128)[:, :, 0:512]
        dstx = xns[:].rearrange("p (c t) -> p c t", t=512)
        sxl[0].inc(sync.dma_start(out=dstx, in_=srcx), 16)

    @ctx.blk.gpsimd
    def _(g):
        for tb in range(4):
            for e in range(8):
                for oc2 in range(8):
                    nwi = (tb * 8 + e) * 8 + oc2
                    if nwi >= 2:
                        g.wait_ge(sgup.h, (nwi - 1) * 2)
                    srcw = gup_bf_d[e * 2048:(e + 1) * 2048, oc2 * 256:(oc2 + 1) * 256]
                    srcw = srcw.rearrange("(k p) o -> p k o", p=128)
                    dstw = gws[:, (nwi % 2) * 16 * 256:(nwi % 2 + 1) * 16 * 256].rearrange(
                        "p (k o) -> p k o", o=256)
                    sgw[nwi % 2].inc(g.dma_start(out=dstw, in_=srcw), 16)
            for ocd in range(16):
                nwd = tb * 16 + ocd
                if nwd >= 2:
                    g.wait_ge(sdn.h, nwd - 1)
                srcw = down_bf_d.rearrange("(q p) o -> p q o", p=128)[:, :, ocd * 128:(ocd + 1) * 128]
                dstw = dws[:, (nwd % 2) * 64 * 128:(nwd % 2 + 1) * 64 * 128].rearrange(
                    "p (q o) -> p q o", o=128)
                sdw[nwd % 2].inc(g.dma_start(out=dstw, in_=srcw), 16)

    ngup = [0]
    ndn = [0]
    for tb in range(4):
        jx = 0
        xb = 0

        @ctx.blk.sync
        def _(sync, tb=tb):
            if tb > 0:
                sync.wait_ge(ctx.sem(tag + "_rq").h, 8 * tb)
            srcx = xn32_dram.rearrange("(c p) t -> p c t", p=128)[:, 0:2, tb * 512:(tb + 1) * 512]
            dstx = xns32[:].rearrange("p (c t) -> p c t", t=512)
            ctx.sem(tag + "_rl").inc(sync.dma_start(out=dstx, in_=srcx), 16)

        for kq in range(8):
            @ctx.blk.tensor
            def _(t, tb=tb, kq=kq, stev_n=stev.n, sgev_n=sgev.n):
                if kq == 0:
                    scst.wait(t)
                    crdy.wait(t)
                    if tb > 0:
                        t.wait_ge(srt.h, 4 * tb)
                        t.wait_ge(stev.h, stev_n)
                        t.wait_ge(sgev.h, sgev_n)
                t.wait_ge(ctx.sem(tag + "_rl").h, (8 * tb + kq + 1) * 16)
                for k2 in range(2):
                    k = kq * 2 + k2
                    t.matmul(mm[1][0:16, :], glaT[:, k * 16:(k + 1) * 16],
                             xns32[:, k2 * 512:(k2 + 1) * 512],
                             start=(k == 0), stop=(k == 15))
                for k2 in range(2):
                    k = kq * 2 + k2
                    i1 = t.matmul(mm[0][0:8, :], gwT[:, k * 8:(k + 1) * 8],
                                  xns32[:, k2 * 512:(k2 + 1) * 512],
                                  start=(k == 0), stop=False)
                ctx.sem(tag + "_rq").inc(i1, 1)

            if kq < 7:
                @ctx.blk.sync
                def _(sync, tb=tb, kq=kq):
                    sync.wait_ge(ctx.sem(tag + "_rq").h, 8 * tb + kq + 1)
                    srcx = xn32_dram.rearrange("(c p) t -> p c t", p=128)[:, 2 * (kq + 1):2 * (kq + 2), tb * 512:(tb + 1) * 512]
                    dstx = xns32[:].rearrange("p (c t) -> p c t", t=512)
                    ctx.sem(tag + "_rl").inc(sync.dma_start(out=dstx, in_=srcx), 16)

        @ctx.blk.vector
        def _(v, tb=tb):
            v.wait_ge(ctx.sem(tag + "_rq").h, 8 * tb + 8)
            vch.step(v, v.tensor_scalar(tg32[:], mm[1][0:16, :], SC, None, ALU.mult))
            srt.inc(v.nop(), 1)

        @ctx.blk.tensor
        def _(t, tb=tb):
            t.wait_ge(srt.h, 4 * tb + 1)
            i1 = t.matmul(mm[0][0:8, :], glbT[:], tg32[:], start=False, stop=True)
            srt.inc(i1, 1)

        @ctx.blk.vector
        def _(v, tb=tb):
            v.wait_ge(srt.h, 4 * tb + 2)
            vch.step(v, v.tensor_copy(lgf[:], mm[0][0:8, :]))
            srt.inc(v.nop(), 2)

        for q in range(4):
            @ctx.blk.tensor
            def _(t, tb=tb, q=q):
                t.wait_ge(srt.h, 4 * tb + 4)
                t.wait_ge(sev2.h, str2.n)
                str2.inc(t.transpose(ctx.ps["tr32"][0:128, 0:8], lgf[:, q * 128:(q + 1) * 128],
                                     ctx.sb["ident_f"][0:8, 0:8]), 1)

            @ctx.blk.vector
            def _(v, tb=tb, q=q):
                v.wait_ge(str2.h, str2.n)
                sev2.inc(v.tensor_copy(lgtm[:, q * 8:(q + 1) * 8], ctx.ps["tr32"][0:128, 0:8]), 1)

        for q in range(4):
            @ctx.blk.vector
            def _(v, tb=tb, q=q):
                v.wait_ge(sev2.h, sev2.n)
                lg = lgtm[:, q * 8:(q + 1) * 8]
                pr = prb[:, q * 8:(q + 1) * 8]
                vch.step(v, v.reduce_max(ccol[:, 0:1], lg, axis=AX.X))
                vch.step(v, v.tensor_scalar(pr, lg, ccol[:, 0:1], None, ALU.subtract))
                ctx.sem(tag + "_sm").inc(v.nop(), 1)
                v.wait_ge(ctx.sem(tag + "_sme").h, 4 * tb + q + 1)
                vch.step(v, v.reduce_sum(ccol[:, 1:2], pr, axis=AX.X))
                vch.step(v, v.reciprocal(ccol[:, 1:2], ccol[:, 1:2]))
                vch.step(v, v.tensor_scalar(pr, pr, ccol[:, 1:2], None, ALU.mult))
                vch.step(v, v.reduce_max(ccol[:, 2:3], pr, axis=AX.X))
                vch.step(v, v.tensor_scalar(msk[:], pr, ccol[:, 2:3], None, ALU.is_equal))
                vch.step(v, v.tensor_mul(cwt[:, q * 8:(q + 1) * 8], msk[:], pr))
                vch.step(v, v.tensor_sub(cwt[:, q * 8:(q + 1) * 8], pr, cwt[:, q * 8:(q + 1) * 8]))
                vch.step(v, v.reduce_max(ccol[:, 3:4], cwt[:, q * 8:(q + 1) * 8], axis=AX.X))
                vch.step(v, v.tensor_scalar(cwt[:, q * 8:(q + 1) * 8], cwt[:, q * 8:(q + 1) * 8],
                                            ccol[:, 3:4], None, ALU.is_equal))
                vch.step(v, v.tensor_add(cwt[:, q * 8:(q + 1) * 8], cwt[:, q * 8:(q + 1) * 8], msk[:]))
                vch.step(v, v.tensor_add(ccol[:, 4:5], ccol[:, 2:3], ccol[:, 3:4]))
                vch.step(v, v.reciprocal(ccol[:, 4:5], ccol[:, 4:5]))
                vch.step(v, v.tensor_mul(cwt[:, q * 8:(q + 1) * 8], cwt[:, q * 8:(q + 1) * 8], pr))
                vch.step(v, v.tensor_scalar(cwt[:, q * 8:(q + 1) * 8], cwt[:, q * 8:(q + 1) * 8],
                                            ccol[:, 4:5], None, ALU.mult))
                ctx.sem(tag + "_c").inc(v.nop(), 1)

            @ctx.blk.scalar
            def _(s, tb=tb, q=q):
                s.wait_ge(ctx.sem(tag + "_sm").h, 4 * tb + q + 1)
                sch.step(s, s.activation(prb[:, q * 8:(q + 1) * 8], prb[:, q * 8:(q + 1) * 8], AF.Exp))
                ctx.sem(tag + "_sme").inc(s.nop(), 1)

        for e in range(8):
            for q in range(4):
                @ctx.blk.tensor
                def _(t, tb=tb, e=e, q=q):
                    t.wait_ge(ctx.sem(tag + "_c").h, 4 * tb + 4)
                    t.wait_ge(sev2.h, str2.n)
                    str2.inc(t.transpose(ctx.ps["tr32"][0:1, 0:128],
                                         cwt[:, q * 8 + e:q * 8 + e + 1],
                                         ctx.sb["ident_f"][:]), 1)

                @ctx.blk.vector
                def _(v, e=e, q=q):
                    v.wait_ge(str2.h, str2.n)
                    sev2.inc(v.tensor_copy(crow[:, q * 128:(q + 1) * 128],
                                           ctx.ps["tr32"][0:1, 0:128]), 1)

            @ctx.blk.tensor
            def _(t, tb=tb, e=e):
                t.wait_ge(sev2.h, sev2.n)
                if tb > 0 or e > 0:
                    t.wait_ge(ctx.sem(tag + "_cbe").h, 8 * tb + e)
                str2.inc(t.matmul(mm[2][:], ctx.sb["ones_row_f"][:], crow[:],
                                  start=True, stop=True), 1)

            @ctx.blk.vector
            def _(v, tb=tb, e=e):
                v.wait_ge(str2.h, str2.n)
                vch.step(v, v.tensor_copy(cbst[:, e * 512:(e + 1) * 512], mm[2][:]))
                sev2.inc(v.nop(), 1)
                ctx.sem(tag + "_cbe").inc(v.nop(), 1)

        for e in range(8):
            @ctx.blk.tensor
            def _(t, tb=tb, e=e, jx=jx, xb=xb):
                t.wait_ge(sxl[0].h, (tb + 1) * 16)
                t.wait_ge(srt.h, 4 * tb + 4)
                t.wait_ge(stev.h, 8 * tb + e)
                for k in range(16):
                    i1 = t.matmul(mm[1][0:16, :], gAT[:, k * 128 + e * 16: k * 128 + e * 16 + 16],
                                  xns[:, xb + k * 512: xb + (k + 1) * 512],
                                  start=(k == 0), stop=(k == 15))
                ster.inc(i1, 1)

            @ctx.blk.vector
            def _(v, tb=tb, e=e, ster_n=ster.n, sgup_n=sgup.n):
                v.wait_ge(ster.h, ster_n)
                v.wait_ge(sgup.h, sgup_n)
                vch.step(v, v.tensor_scalar(tg, mm[1][0:16, :], SC, None, ALU.mult))
                stev.inc(v.nop(), 1)

            for oc in range(16):
                n = ngup[0]; ngup[0] += 1
                nwi = (tb * 8 + e) * 8 + oc // 2

                @ctx.blk.tensor
                def _(t, tb=tb, e=e, oc=oc, n=n, nwi=nwi, xb=xb, stev_n=stev.n):
                    if oc % 2 == 0:
                        t.wait_ge(sgw[nwi % 2].h, (nwi // 2 + 1) * 16)
                    t.wait_ge(stev.h, stev_n)
                    if n >= 2:
                        t.wait_ge(sgev.h, n - 1)
                    base = (nwi % 2) * 16 * 256
                    gb = mm[0] if n % 2 == 0 else mm[3]
                    for k in range(16):
                        t.matmul(gb[:], gws[:, base + k * 256 + (oc % 2) * 128: base + k * 256 + (oc % 2) * 128 + 128],
                                 xns[:, xb + k * 512: xb + (k + 1) * 512],
                                 start=(k == 0), stop=False)
                    i1 = t.matmul(gb[:], gBT[0:16, e * 2048 + oc * 128: e * 2048 + (oc + 1) * 128],
                                  tg, start=False, stop=True)
                    sgup.inc(i1, 1)

                @ctx.blk.scalar
                def _(s, tb=tb, e=e, oc=oc, n=n):
                    if oc < 8:
                        s.wait_ge(sgup.h, n + 1)
                        s.wait_ge(ctx.sem(tag + "_cbe").h, 8 * tb + 8)
                        sch.step(s, s.activation(gsil[:, oc * 512:(oc + 1) * 512],
                                                 (mm[0] if n % 2 == 0 else mm[3])[:], AF.Silu))
                        ctx.sem(tag + "_gsl").inc(s.nop(), 1)

                @ctx.blk.vector
                def _(v, tb=tb, e=e, oc=oc, n=n):
                    if oc < 8:
                        v.wait_ge(ctx.sem(tag + "_gsl").h, (8 * tb + e) * 8 + oc + 1)
                        vch.step(v, v.tensor_mul(gsil[:, oc * 512:(oc + 1) * 512],
                                                 gsil[:, oc * 512:(oc + 1) * 512],
                                                 cbst[:, e * 512:(e + 1) * 512]))
                        sgev.inc(v.nop(), 1)
                    else:
                        v.wait_ge(sgup.h, n + 1)
                        v.wait_ge(ctx.sem(tag + "_gsl").h, (8 * tb + e) * 8 + 8)
                        if tb > 0:
                            v.wait_ge(sdn.h, 16 * tb)
                        vch.step(v, v.tensor_tensor(hstk[:, (e * 8 + (oc - 8)) * 512:(e * 8 + (oc - 8) + 1) * 512],
                                                    (mm[0] if n % 2 == 0 else mm[3])[:], gsil[:, (oc - 8) * 512:(oc - 7) * 512],
                                                    ALU.mult))
                        sgev.inc(v.nop(), 1)

            @ctx.blk.tensor
            def _(t, tb=tb, e=e, sgev_n=sgev.n, stev_n=stev.n, sgup_n=sgup.n):
                t.wait_ge(sgev.h, sgev_n)
                t.wait_ge(stev.h, stev_n)
                t.wait_ge(sgup.h, sgup_n)
                for k in range(8):
                    i1 = t.matmul(mm[1][32:48, :], dAT[:, k * 128 + e * 16: k * 128 + e * 16 + 16],
                                  hstk[:, (e * 8 + k) * 512:(e * 8 + k + 1) * 512],
                                  start=(k == 0), stop=(k == 7), tile_position=(0, 32))
                ster.inc(i1, 1)

            @ctx.blk.vector
            def _(v, tb=tb, e=e, ster_n=ster.n):
                v.wait_ge(ster.h, ster_n)
                vch.step(v, v.tensor_scalar(tdn[:, e * 512:(e + 1) * 512], mm[1][32:48, :], SC, None, ALU.mult))
                stev.inc(v.nop(), 1)

        for oc in range(16):
            n = ndn[0]; ndn[0] += 1
            nwd = tb * 16 + oc

            @ctx.blk.tensor
            def _(t, tb=tb, oc=oc, n=n, nwd=nwd):
                t.wait_ge(sdw[nwd % 2].h, (nwd // 2 + 1) * 16)
                t.wait_ge(stev.h, (tb + 1) * 16)
                if n >= 2:
                    t.wait_ge(sdev.h, n - 1)
                base = (nwd % 2) * 64 * 128
                for e in range(8):
                    for k in range(8):
                        t.matmul(mm[2 + n % 2][:],
                                 dws[:, base + (e * 8 + k) * 128: base + (e * 8 + k) * 128 + 128],
                                 hstk[:, (e * 8 + k) * 512:(e * 8 + k + 1) * 512],
                                 start=(e == 0 and k == 0), stop=False)
                for e in range(8):
                    i1 = t.matmul(mm[2 + n % 2][:], dBT[0:16, e * 2048 + oc * 128: e * 2048 + (oc + 1) * 128],
                                  tdn[:, e * 512:(e + 1) * 512], start=False, stop=(e == 7))
                sdn.inc(i1, 1)

            @ctx.blk.vector
            def _(v, oc=oc, n=n):
                v.wait_ge(sdn.h, n + 1)
                if n >= 2:
                    v.wait_ge(sstf[n % 2].h, (n // 2) * 16)
                sdev.inc(v.tensor_copy(ob[:, (n % 2) * 512:(n % 2) * 512 + 512], mm[2 + n % 2][:]), 1)

            @ctx.blk.sync
            def _(sync, tb=tb, oc=oc, n=n):
                sync.wait_ge(sdev.h, n + 1)
                dst = f1_dram[oc * 128:(oc + 1) * 128, tb * 512:(tb + 1) * 512]
                sstf[n % 2].inc(sync.dma_start(out=dst, in_=ob[:, (n % 2) * 512:(n % 2) * 512 + 512]), 16)
                if oc == 15 and tb + 1 < 4:
                    sync.wait_ge(sgup.h, ngup[0])
                    srcx = xn_dram.rearrange("(c p) t -> p c t", p=128)[:, :, (tb + 1) * 512:(tb + 2) * 512]
                    dstx = xns[:].rearrange("p (c t) -> p c t", t=512)
                    sxl[0].inc(sync.dma_start(out=dstx, in_=srcx), 16)
    return sstf


def emit_castbounce(ctx, tag, src_dram, dst_dram, rows, cols, nchunk=8):
    ss = [ctx.sem(tag + f"_cb{j}") for j in range(2)]
    step = rows // nchunk

    @ctx.blk.gpsimd
    def _(g):
        for i in range(nchunk):
            j = i % 2
            if i >= 2:
                g.wait_ge(ss[j].h, (i // 2) * 16)
            ss[j].inc(g.dma_start(out=dst_dram[i * step:(i + 1) * step, :],
                                  in_=src_dram[i * step:(i + 1) * step, :]), 16)
    return ss


def emit_addfm(ctx, tag, a_dram, b_dram, out_dram):
    nc = ctx.nc
    sld = [ctx.sem(tag + f"_ld{j}") for j in range(2)]
    sstq = [ctx.sem(tag + f"_st{j}") for j in range(2)]
    sadd = ctx.sem(tag + "_add")
    buf = nc.alloc_sbuf_tensor(tag + "_b", [128, 2 * 2 * T], BF16)
    ob = nc.alloc_sbuf_tensor(tag + "_o", [128, 2 * T], BF16)
    vch = ctx.chain(tag + "_v")

    for i in range(NDC):
        j = i % 2

        @ctx.blk.sync
        def _(sync, i=i, j=j):
            if i >= 2:
                sync.wait_ge(sadd.h, i - 1)
            sld[j].inc(sync.dma_start(out=buf[:, j * 2 * T:j * 2 * T + T],
                                      in_=a_dram[i * 128:(i + 1) * 128, :]), 16)
            sld[j].inc(sync.dma_start(out=buf[:, j * 2 * T + T:j * 2 * T + 2 * T],
                                      in_=b_dram[i * 128:(i + 1) * 128, :]), 16)

        @ctx.blk.vector
        def _(v, i=i, j=j):
            v.wait_ge(sld[j].h, (i // 2 + 1) * 32)
            if i >= 2:
                v.wait_ge(sstq[j].h, (i // 2) * 16)
            vch.step(v, v.tensor_add(ob[:, j * T:(j + 1) * T], buf[:, j * 2 * T:j * 2 * T + T],
                                     buf[:, j * 2 * T + T:j * 2 * T + 2 * T]))
            sadd.inc(v.nop(), 1)

        @ctx.blk.sync
        def _(sync, i=i, j=j):
            sync.wait_ge(sadd.h, i + 1)
            sstq[j].inc(sync.dma_start(out=out_dram[i * 128:(i + 1) * 128, :],
                                       in_=ob[:, j * T:(j + 1) * T]), 16)
    return sstq


def emit_final(ctx, tag, x_dram, m_drams, out_dram):
    nc = ctx.nc
    NM = len(m_drams)
    sld = [ctx.sem(tag + f"_ld{j}") for j in range(2)]
    sstq = [ctx.sem(tag + f"_st{j}") for j in range(2)]
    str_ = ctx.sem(tag + "_tr")
    sev = ctx.sem(tag + "_ev")
    sacc = ctx.sem(tag + "_acc")
    xa = nc.alloc_sbuf_tensor(tag + "_xa", [128, 2 * D], F32)
    ma = nc.alloc_sbuf_tensor(tag + "_ma", [128, 2 * NM * D], BF16)
    acc = nc.alloc_sbuf_tensor(tag + "_acc", [128, 2 * D], F32)
    trb = ctx.ps["trb"]
    crdy = ctx.sem("const_rdy")
    vch = ctx.chain(tag + "_v")

    def issue_load(sync, i):
        j = i % 2
        sld[j].inc(sync.dma_start(out=xa[:, j * D:(j + 1) * D],
                                  in_=x_dram[i * 128:(i + 1) * 128, :]), 16)
        for mi, md in enumerate(m_drams):
            msrc = md.rearrange("(c p) t -> p c t", p=128)[:, :, i * 128:(i + 1) * 128]
            mdst = ma[:, (j * NM + mi) * D:(j * NM + mi + 1) * D].rearrange("p (c f) -> p c f", f=128)
            sld[j].inc(sync.dma_start(out=mdst, in_=msrc), 16)

    @ctx.blk.sync
    def _(sync):
        issue_load(sync, 0)
        issue_load(sync, 1)

    for i in range(NTT):
        j = i % 2

        @ctx.blk.tensor
        def _(t, i=i, j=j):
            t.wait_ge(sld[j].h, (i // 2 + 1) * 16 * (1 + NM))
            crdy.wait(t)
            for mi in range(NM):
                for c in range(NDC):
                    kg = (i * NM + mi) * NDC + c
                    if kg >= 4:
                        t.wait_ge(sev.h, kg - 3)
                    ins = t.transpose(trb[kg % 2][:, (c % 2) * 128 + (kg // 2 % 2) * 256:
                                                  (c % 2) * 128 + (kg // 2 % 2) * 256 + 128],
                                      ma[:, (j * NM + mi) * D + c * 128:(j * NM + mi) * D + (c + 1) * 128],
                                      ctx.sb["ident_bf"][:])
                    str_.inc(ins, 1)

        @ctx.blk.vector
        def _(v, i=i, j=j):
            v.wait_ge(sld[j].h, (i // 2 + 1) * 16 * (1 + NM))
            if i >= 2:
                v.wait_ge(sstq[j].h, (i // 2) * 16)
            first = True
            for mi in range(NM):
                for c in range(NDC):
                    kg = (i * NM + mi) * NDC + c
                    v.wait_ge(str_.h, kg + 1)
                    srcp = trb[kg % 2][:, (c % 2) * 128 + (kg // 2 % 2) * 256:
                                       (c % 2) * 128 + (kg // 2 % 2) * 256 + 128]
                    if first:
                        ins = v.tensor_add(acc[:, j * D + c * 128:j * D + (c + 1) * 128],
                                           xa[:, j * D + c * 128:j * D + (c + 1) * 128], srcp)
                    else:
                        ins = v.tensor_add(acc[:, j * D + c * 128:j * D + (c + 1) * 128],
                                           acc[:, j * D + c * 128:j * D + (c + 1) * 128], srcp)
                    vch.sem.inc(ins, 1)
                    v.wait_ge(vch.sem.h, vch.sem.n)
                    sev.inc(v.nop(), 1)
                first = False
            sacc.inc(v.nop(), 1)

        @ctx.blk.sync
        def _(sync, i=i, j=j):
            if i + 2 < NTT:
                sync.wait_ge(sacc.h, i + 1)
                issue_load(sync, i + 2)
            sync.wait_ge(sacc.h, i + 1)
            sstq[j].inc(sync.dma_start(out=out_dram[i * 128:(i + 1) * 128, :],
                                       in_=acc[:, j * D:(j + 1) * D]), 16)
    return sstq


def _finish(blk, sems):
    @blk.sync
    def _(sync):
        for s in sems:
            s.wait(sync)


def kernel(**inputs):
    g = {k: np.asarray(v) for k, v in inputs.items()}
    x_full = g["L0_attn_x"].astype(np.float32)
    E8, R16, I2 = 8, 16, 1024
    CD = 6144

    ipT32 = np.ascontiguousarray(g["L1_attn_in_proj_w"].astype(np.float32).T)
    opT32 = np.ascontiguousarray(g["L1_attn_out_proj_w"].astype(np.float32).T)
    gup32 = np.ascontiguousarray(g["L1_ffn_gate_up_proj"].astype(np.float32).reshape(E8 * D, 2 * I2))
    dwn32 = np.ascontiguousarray(g["L1_ffn_down_proj"].astype(np.float32).reshape(E8 * I2, D))
    gwT = np.ascontiguousarray(g["L1_ffn_gate_weight"].astype(np.float32).T)
    glaT = np.ascontiguousarray(g["L1_ffn_gate_lora_A"].astype(np.float32).T)
    glbT = np.ascontiguousarray(g["L1_ffn_gate_lora_B"].astype(np.float32).T)
    gAT = np.ascontiguousarray(g["L1_ffn_gup_lora_A"].astype(np.float32).transpose(2, 0, 1).reshape(D, E8 * R16))
    gBT = np.ascontiguousarray(g["L1_ffn_gup_lora_B"].astype(np.float32).transpose(2, 0, 1).reshape(R16, E8 * D))
    dAT = np.ascontiguousarray(g["L1_ffn_down_lora_A"].astype(np.float32).transpose(2, 0, 1).reshape(I2, E8 * R16))
    dBT = np.ascontiguousarray(g["L1_ffn_down_lora_B"].astype(np.float32).transpose(2, 0, 1).reshape(R16, E8 * D))
    cw_l = np.ascontiguousarray(g["L1_attn_conv_w"].astype(np.float32).reshape(CD, 4)
                                .reshape(48, 128, 4).transpose(1, 0, 2).reshape(128, 192))
    cb_l = np.ascontiguousarray(g["L1_attn_conv_b"].astype(np.float32).reshape(48, 128).T)
    dtb = np.ascontiguousarray(g["L1_attn_dt_bias"].astype(np.float32).reshape(32, 1))
    al = np.ascontiguousarray(g["L1_attn_A_log"].astype(np.float32).reshape(32, 1))
    dD_ = np.ascontiguousarray(g["L1_attn_D"].astype(np.float32).reshape(32, 1))
    nw_l = np.ascontiguousarray(g["L1_attn_norm_w"].astype(np.float32).reshape(32, 128).T)
    lq1 = g["L0_attn_lambda_q1"].astype(np.float32).reshape(1, 64)
    lk1 = g["L0_attn_lambda_k1"].astype(np.float32).reshape(1, 64)
    lq2 = g["L0_attn_lambda_q2"].astype(np.float32).reshape(1, 64)
    lk2 = g["L0_attn_lambda_k2"].astype(np.float32).reshape(1, 64)
    rms = g["L0_attn_rms_scale"].astype(np.float32).reshape(1, 128)
    xo = [np.ascontiguousarray(x_full[c // 2, (c % 2) * T:(c % 2 + 1) * T, :]) for c in range(8)]
    flags = [np.full((128, 1), float(c % 2), np.float32) for c in range(8)]
    C8 = list(range(8))

    def newprog():
        nc = bass.Bass("TRN2", target_bir_lowering=False, debug=False)
        return nc

    nc = newprog()
    wfc_d = nc.dram_tensor("wfc", [8192, D], F32, kind="ExternalInput").ap()
    wpr_d = nc.dram_tensor("wpr", [D, 8192], F32, kind="ExternalInput").ap()
    ip32_d = nc.dram_tensor("ip32", [D, 10272], F32, kind="ExternalInput").ap()
    op32_d = nc.dram_tensor("op32", [4096, D], F32, kind="ExternalInput").ap()
    gu32_d = nc.dram_tensor("gu32", [E8 * D, 2 * I2], F32, kind="ExternalInput").ap()
    dw32_d = nc.dram_tensor("dw32", [E8 * I2, D], F32, kind="ExternalInput").ap()
    wq1_d = nc.dram_tensor("wq1", [8192, D], BF16, kind="ExternalOutput").ap()
    wq2_d = nc.dram_tensor("wq2", [D, 8192], BF16, kind="ExternalOutput").ap()
    w1T_d = nc.dram_tensor("w1T", [D, 8192], BF16, kind="ExternalOutput").ap()
    w2T_d = nc.dram_tensor("w2T", [8192, D], BF16, kind="ExternalOutput").ap()
    ipb_d = nc.dram_tensor("ipb", [D, 10272], BF16, kind="ExternalOutput").ap()
    opb_d = nc.dram_tensor("opb", [4096, D], BF16, kind="ExternalOutput").ap()
    gub_d = nc.dram_tensor("gub", [E8 * D, 2 * I2], BF16, kind="ExternalOutput").ap()
    dwb_d = nc.dram_tensor("dwb", [E8 * I2, D], BF16, kind="ExternalOutput").ap()
    with nc.Block() as blk:
        ctx = Ctx(nc, blk)
        alloc_common(ctx)
        s1 = emit_quant(ctx, "q1", wfc_d, wq1_d, 8192, D)
        s2 = emit_quant(ctx, "q2", wpr_d, wq2_d, D, 8192)
        s3 = emit_transpose_mat(ctx, "t1", wq1_d, w1T_d, 8192, D, gate_sems=s1)
        s4 = emit_transpose_mat(ctx, "t2", wq2_d, w2T_d, D, 8192, gate_sems=s2)
        s5 = emit_castbounce(ctx, "c1", ip32_d, ipb_d, D, 10272)
        s6 = emit_castbounce(ctx, "c2", op32_d, opb_d, 4096, D)
        s7 = emit_castbounce(ctx, "c3", gu32_d, gub_d, E8 * D, 2 * I2)
        s8c = emit_castbounce(ctx, "c4", dw32_d, dwb_d, E8 * I2, D)
        _finish(blk, list(s1) + list(s2) + list(s3) + list(s4) + s5 + s6 + s7 + s8c)
    r = run_bass_kernel_spmd(nc, [dict(
        wfc=g["L0_ffn_W_fc"].astype(np.float32), wpr=g["L0_ffn_W_proj"].astype(np.float32),
        ip32=ipT32, op32=opT32, gu32=gup32, dw32=dwn32)], core_ids=[0]).results[0]
    w1T, w2T, ipb, opb, gub, dwb = r["w1T"], r["w2T"], r["ipb"], r["opb"], r["gub"], r["dwb"]

    nc = newprog()
    x_d = nc.dram_tensor("x", [T, D], F32, kind="ExternalInput").ap()
    xn_d = nc.dram_tensor("xnT", [D, T], BF16, kind="ExternalOutput").ap()
    with nc.Block() as blk:
        ctx = Ctx(nc, blk)
        alloc_common(ctx)
        _finish(blk, emit_ln(ctx, "ln0", x_d, xn_d))
    rs = run_bass_kernel_spmd(nc, [dict(x=xo[c]) for c in C8], core_ids=C8).results
    xn0 = [rs[c]["xnT"] for c in C8]

    nc = newprog()
    xn_d = nc.dram_tensor("xnT", [D, T], BF16, kind="ExternalInput").ap()
    w_d = nc.dram_tensor("w_qkv", [D, 3 * D], F32, kind="ExternalInput").ap()
    q_d = nc.dram_tensor("q", [T, D], BF16, kind="ExternalOutput").ap()
    kv_d = nc.dram_tensor("kv", [T, 2 * D], BF16, kind="ExternalOutput").ap()
    with nc.Block() as blk:
        ctx = Ctx(nc, blk)
        alloc_common(ctx)
        _finish(blk, emit_qkv(ctx, "qkv", xn_d, w_d, q_d, kv_d))
    wqkv = g["L0_attn_w_qkv"].astype(np.float32)
    rs = run_bass_kernel_spmd(nc, [dict(xnT=xn0[c], w_qkv=wqkv) for c in C8], core_ids=C8).results
    qs = [rs[c]["q"] for c in C8]
    kvs = [rs[c]["kv"] for c in C8]
    kfull, v3full = [], []
    for c in C8:
        p0, p1 = (c // 2) * 2, (c // 2) * 2 + 1
        kfull.append(np.ascontiguousarray(np.concatenate([kvs[p0][:, :D], kvs[p1][:, :D]], 0)))
        vf = np.concatenate([kvs[p0][:, D:], kvs[p1][:, D:]], 0)
        v3full.append(np.ascontiguousarray(vf.reshape(2 * T, 16, 128).transpose(1, 0, 2)))

    nc = newprog()
    xn_d = nc.dram_tensor("xnT", [D, T], BF16, kind="ExternalInput").ap()
    w1t_d = nc.dram_tensor("w1T", [D, 8192], BF16, kind="ExternalInput").ap()
    w2t_d = nc.dram_tensor("w2T", [8192, D], BF16, kind="ExternalInput").ap()
    f0_d = nc.dram_tensor("f0", [D, T], BF16, kind="ExternalOutput").ap()
    with nc.Block() as blk:
        ctx = Ctx(nc, blk)
        alloc_common(ctx)
        _finish(blk, emit_ffn0(ctx, "f0", xn_d, w1t_d, w2t_d, f0_d))
    rs = run_bass_kernel_spmd(nc, [dict(xnT=xn0[c], w1T=w1T, w2T=w2T) for c in C8],
                              core_ids=C8).results
    f0s = [rs[c]["f0"] for c in C8]

    TK = 2 * T
    nc = newprog()
    q_d = nc.dram_tensor("q", [T, D], BF16, kind="ExternalInput").ap()
    k_d = nc.dram_tensor("k", [TK, D], BF16, kind="ExternalInput").ap()
    v_d = nc.dram_tensor("v3", [16, TK, 128], BF16, kind="ExternalInput").ap()
    lq1_d = nc.dram_tensor("lq1", [1, 64], F32, kind="ExternalInput").ap()
    lk1_d = nc.dram_tensor("lk1", [1, 64], F32, kind="ExternalInput").ap()
    lq2_d = nc.dram_tensor("lq2", [1, 64], F32, kind="ExternalInput").ap()
    lk2_d = nc.dram_tensor("lk2", [1, 64], F32, kind="ExternalInput").ap()
    rms_d = nc.dram_tensor("rms", [1, 128], F32, kind="ExternalInput").ap()
    wp_d = nc.dram_tensor("wp", [D, D], F32, kind="ExternalInput").ap()
    a0_d = nc.dram_tensor("a0", [D, T], BF16, kind="ExternalOutput").ap()
    with nc.Block() as blk:
        ctx = Ctx(nc, blk)
        alloc_common(ctx)
        _finish(blk, emit_attn(ctx, "at", q_d, [(k_d, TK)], [(v_d, TK)],
                               lq1_d, lk1_d, lq2_d, lk2_d, rms_d, wp_d, a0_d))
    wproj = g["L0_attn_w_proj"].astype(np.float32)
    rs = run_bass_kernel_spmd(nc, [dict(q=qs[c], k=kfull[c], v3=v3full[c], lq1=lq1, lk1=lk1,
                                        lq2=lq2, lk2=lk2, rms=rms, wp=wproj) for c in C8],
                              core_ids=C8).results
    a0s = [rs[c]["a0"] for c in C8]

    nc = newprog()
    a_d = nc.dram_tensor("a", [D, T], BF16, kind="ExternalInput").ap()
    b_d = nc.dram_tensor("b", [D, T], BF16, kind="ExternalInput").ap()
    o_d = nc.dram_tensor("m01", [D, T], BF16, kind="ExternalOutput").ap()
    with nc.Block() as blk:
        ctx = Ctx(nc, blk)
        alloc_common(ctx)
        _finish(blk, emit_addfm(ctx, "ad", a_d, b_d, o_d))
    rs = run_bass_kernel_spmd(nc, [dict(a=a0s[c], b=f0s[c]) for c in C8], core_ids=C8).results
    m01s = [rs[c]["m01"] for c in C8]

    nc = newprog()
    x_d = nc.dram_tensor("x", [T, D], F32, kind="ExternalInput").ap()
    m_d = nc.dram_tensor("m01", [D, T], BF16, kind="ExternalInput").ap()
    xn_d = nc.dram_tensor("xnT", [D, T], BF16, kind="ExternalOutput").ap()
    x32_d = nc.dram_tensor("xn32", [T, D], F32, kind="ExternalOutput").ap()
    with nc.Block() as blk:
        ctx = Ctx(nc, blk)
        alloc_common(ctx)
        _finish(blk, emit_ln(ctx, "ln1", x_d, xn_d, m0_dram=m_d, out32_dram=x32_d))
    rs = run_bass_kernel_spmd(nc, [dict(x=xo[c], m01=m01s[c]) for c in C8], core_ids=C8).results
    xn1 = [rs[c]["xnT"] for c in C8]
    xn32fm = [np.ascontiguousarray(rs[c]["xn32"].T) for c in C8]

    nc = newprog()
    xn_d = nc.dram_tensor("xnT", [D, T], BF16, kind="ExternalInput").ap()
    w_d = nc.dram_tensor("ipb", [D, 10272], BF16, kind="ExternalInput").ap()
    g_d = nc.dram_tensor("gate", [4096, T], BF16, kind="ExternalOutput").ap()
    h_d = nc.dram_tensor("hbc", [CD, T], BF16, kind="ExternalOutput").ap()
    dt_d = nc.dram_tensor("dtp", [32, T], F32, kind="ExternalOutput").ap()
    with nc.Block() as blk:
        ctx = Ctx(nc, blk)
        alloc_common(ctx)
        _finish(blk, emit_proj_g(ctx, "ip", xn_d, w_d,
                                 [(g_d, 0, 4096), (h_d, 0, CD), (dt_d, 0, 32, F32)], D))
    rs = run_bass_kernel_spmd(nc, [dict(xnT=xn1[c], ipb=ipb) for c in C8], core_ids=C8).results
    gates = [rs[c]["gate"] for c in C8]
    hbcs = [rs[c]["hbc"] for c in C8]
    dtps = [rs[c]["dtp"] for c in C8]
    tails = [np.zeros((CD, 3), bf16) if c % 2 == 0
             else np.ascontiguousarray(hbcs[c - 1][:, T - 3:T]) for c in C8]

    nc = newprog()
    hbc_d = nc.dram_tensor("hbc", [CD, T], BF16, kind="ExternalInput").ap()
    dtp_d = nc.dram_tensor("dtp", [32, T], F32, kind="ExternalInput").ap()
    tail_d = nc.dram_tensor("tail", [CD, 3], BF16, kind="ExternalInput").ap()
    flag_d = nc.dram_tensor("flag", [128, 1], F32, kind="ExternalInput").ap()
    cw_d = nc.dram_tensor("cw", [128, 192], F32, kind="ExternalInput").ap()
    cb_d = nc.dram_tensor("cb", [128, 48], F32, kind="ExternalInput").ap()
    dtb_d = nc.dram_tensor("dtb", [32, 1], F32, kind="ExternalInput").ap()
    al_d = nc.dram_tensor("al", [32, 1], F32, kind="ExternalInput").ap()
    st_d = nc.dram_tensor("stash", [CD, T], BF16, kind="ExternalOutput").ap()
    dtT_d = nc.dram_tensor("dtT", [32, T], F32, kind="ExternalOutput").ap()
    cst_d = nc.dram_tensor("cst", [8 * 128, 4096], F32, kind="ExternalOutput").ap()
    s8_d = nc.dram_tensor("s8", [128, 4096], F32, kind="ExternalOutput").ap()
    with nc.Block() as blk:
        ctx = Ctx(nc, blk)
        alloc_common(ctx)
        _finish(blk, emit_l1a(ctx, "la", hbc_d, dtp_d, tail_d, flag_d, cw_d, cb_d,
                              dtb_d, al_d, st_d, dtT_d, cst_d, s8_d))
    rs = run_bass_kernel_spmd(nc, [dict(hbc=hbcs[c], dtp=dtps[c], tail=tails[c], flag=flags[c],
                                        cw=cw_l, cb=cb_l, dtb=dtb, al=al) for c in C8],
                              core_ids=C8).results
    stashes = [rs[c]["stash"] for c in C8]
    dtTs = [rs[c]["dtT"] for c in C8]
    csts = [rs[c]["cst"] for c in C8]
    s8s = [rs[c]["s8"] for c in C8]
    s8ins = [np.zeros((128, 4096), np.float32) if c % 2 == 0 else s8s[c - 1] for c in C8]

    nc = newprog()
    st_d = nc.dram_tensor("stash", [CD, T], BF16, kind="ExternalInput").ap()
    dt_d = nc.dram_tensor("dtT", [32, T], F32, kind="ExternalInput").ap()
    gt_d = nc.dram_tensor("gate", [4096, T], BF16, kind="ExternalInput").ap()
    cst_d = nc.dram_tensor("cst", [8 * 128, 4096], F32, kind="ExternalInput").ap()
    s8_d = nc.dram_tensor("s8in", [128, 4096], F32, kind="ExternalInput").ap()
    fl_d = nc.dram_tensor("flag", [128, 1], F32, kind="ExternalInput").ap()
    nw_d = nc.dram_tensor("nw", [128, 32], F32, kind="ExternalInput").ap()
    dD_d = nc.dram_tensor("dD", [32, 1], F32, kind="ExternalInput").ap()
    al_d = nc.dram_tensor("al", [32, 1], F32, kind="ExternalInput").ap()
    n1_d = nc.dram_tensor("norm1", [4096, T], BF16, kind="ExternalOutput").ap()
    with nc.Block() as blk:
        ctx = Ctx(nc, blk)
        alloc_common(ctx)
        _finish(blk, emit_l1b(ctx, "lb", st_d, dt_d, gt_d, cst_d, s8_d, fl_d, nw_d, dD_d,
                              al_d, n1_d))
    rs = run_bass_kernel_spmd(nc, [dict(stash=stashes[c], dtT=dtTs[c], gate=gates[c],
                                        cst=csts[c], s8in=s8ins[c], flag=flags[c],
                                        nw=nw_l, dD=dD_, al=al) for c in C8],
                              core_ids=C8).results
    norm1s = [rs[c]["norm1"] for c in C8]

    nc = newprog()
    n1_d = nc.dram_tensor("n1", [4096, T], BF16, kind="ExternalInput").ap()
    w_d = nc.dram_tensor("opb", [4096, D], BF16, kind="ExternalInput").ap()
    a1_d = nc.dram_tensor("a1", [D, T], BF16, kind="ExternalOutput").ap()
    with nc.Block() as blk:
        ctx = Ctx(nc, blk)
        alloc_common(ctx)
        _finish(blk, emit_proj_g(ctx, "op", n1_d, w_d, [(a1_d, 0, D)], 4096))
    rs = run_bass_kernel_spmd(nc, [dict(n1=norm1s[c], opb=opb) for c in C8], core_ids=C8).results
    a1s = [rs[c]["a1"] for c in C8]

    nc = newprog()
    xn_d = nc.dram_tensor("xnT", [D, T], BF16, kind="ExternalInput").ap()
    xn32_d = nc.dram_tensor("xn32", [D, T], F32, kind="ExternalInput").ap()
    gwT_d = nc.dram_tensor("gwT", [D, E8], F32, kind="ExternalInput").ap()
    glaT_d = nc.dram_tensor("glaT", [D, R16], F32, kind="ExternalInput").ap()
    glbT_d = nc.dram_tensor("glbT", [R16, E8], F32, kind="ExternalInput").ap()
    gAT_d = nc.dram_tensor("gAT", [D, 128], F32, kind="ExternalInput").ap()
    gBT_d = nc.dram_tensor("gBT", [R16, E8 * D], F32, kind="ExternalInput").ap()
    dAT_d = nc.dram_tensor("dAT", [I2, 128], F32, kind="ExternalInput").ap()
    dBT_d = nc.dram_tensor("dBT", [R16, E8 * D], F32, kind="ExternalInput").ap()
    gup_d = nc.dram_tensor("gup_bf", [E8 * D, 2 * I2], BF16, kind="ExternalInput").ap()
    dwn_d = nc.dram_tensor("dwn_bf", [E8 * I2, D], BF16, kind="ExternalInput").ap()
    f1_d = nc.dram_tensor("f1", [D, T], BF16, kind="ExternalOutput").ap()
    with nc.Block() as blk:
        ctx = Ctx(nc, blk)
        alloc_common(ctx)
        _finish(blk, emit_moe(ctx, "me", xn_d, xn32_d, gwT_d, glaT_d, glbT_d, gAT_d, gBT_d,
                              dAT_d, dBT_d, gup_d, dwn_d, f1_d))
    rs = run_bass_kernel_spmd(nc, [dict(xnT=xn1[c], xn32=xn32fm[c], gwT=gwT, glaT=glaT,
                                        glbT=glbT, gAT=gAT, gBT=gBT, dAT=dAT, dBT=dBT,
                                        gup_bf=gub, dwn_bf=dwb) for c in C8],
                              core_ids=C8).results
    f1s = [rs[c]["f1"] for c in C8]

    nc = newprog()
    x_d = nc.dram_tensor("x", [T, D], F32, kind="ExternalInput").ap()
    m_ds = [nc.dram_tensor(f"m{i}", [D, T], BF16, kind="ExternalInput").ap() for i in range(3)]
    o_d = nc.dram_tensor("o", [T, D], F32, kind="ExternalOutput").ap()
    with nc.Block() as blk:
        ctx = Ctx(nc, blk)
        alloc_common(ctx)
        _finish(blk, emit_final(ctx, "fi", x_d, m_ds, o_d))
    rs = run_bass_kernel_spmd(nc, [dict(x=xo[c], m0=m01s[c], m1=a1s[c], m2=f1s[c])
                                   for c in C8], core_ids=C8).results
    out = np.empty((4, 2 * T, D), np.float32)
    for c in C8:
        out[c // 2, (c % 2) * T:(c % 2 + 1) * T, :] = rs[c]["o"]
    kernel._DBG = dict(xn0=xn0, qs=qs, kvs=kvs, f0s=f0s, a0s=a0s, m01s=m01s, xn1=xn1,
                       xn32fm=xn32fm, gates=gates, hbcs=hbcs, dtps=dtps, stashes=stashes,
                       dtTs=dtTs, csts=csts, s8s=s8s, norm1s=norm1s, a1s=a1s, f1s=f1s,
                       w1T=w1T, w2T=w2T, ipb=ipb, opb=opb)
    return out


def build_ln_test(with_m0):
    nc = bass.Bass("TRN2", target_bir_lowering=False, debug=False)
    x_d = nc.dram_tensor("x", [T, D], F32, kind="ExternalInput").ap()
    m0_d = nc.dram_tensor("m0", [D, T], BF16, kind="ExternalInput").ap() if with_m0 else None
    out_d = nc.dram_tensor("xnT", [D, T], BF16, kind="ExternalOutput").ap()
    with nc.Block() as blk:
        ctx = Ctx(nc, blk)
        alloc_common(ctx)
        sst = emit_ln(ctx, "ln0", x_d, out_d, m0_dram=m0_d)

        @blk.sync
        def _(sync):
            for s in sst:
                s.wait(sync)
    return nc


def _ln_ref(x, eps=1e-5):
    m = x.mean(-1, keepdims=True)
    v = x.var(-1, keepdims=True)
    return (x - m) / np.sqrt(v + eps)


def test_ln():
    rng = np.random.default_rng(0)
    x = (rng.standard_normal((T, D)) * 2 + 0.3).astype(np.float32)
    m0 = rng.standard_normal((D, T)).astype(bf16)
    for with_m0 in (False, True):
        nc = build_ln_test(with_m0)
        ins = dict(x=x, m0=m0) if with_m0 else dict(x=x)
        res = run_bass_kernel_spmd(nc, [ins], core_ids=[0])
        got = res.results[0]["xnT"].astype(np.float32)
        xin = x.astype(np.float64) + (m0.astype(np.float64).T if with_m0 else 0)
        ref = _ln_ref(xin).T.astype(np.float32)
        err = np.abs(got - ref).max()
        rel = np.sqrt(((got - ref) ** 2).mean() / (ref ** 2).mean())
        print(f"LN(m0={with_m0}): max abs err {err:.4e}  relL2 {rel:.3e}")
        assert rel < 1e-2


def build_quant_test(R, C):
    nc = bass.Bass("TRN2", target_bir_lowering=False, debug=False)
    w_d = nc.dram_tensor("w", [R, C], F32, kind="ExternalInput").ap()
    o_d = nc.dram_tensor("o", [R, C], BF16, kind="ExternalOutput").ap()
    with nc.Block() as blk:
        ctx = Ctx(nc, blk)
        alloc_common(ctx)
        ss = emit_quant(ctx, "q", w_d, o_d, R, C)

        @blk.gpsimd
        def _(g):
            for s in ss:
                s.wait(g)
    return nc


def _ref_ternary(w, gs=64):
    wb = np.asarray(w, np.float32).astype(bf16)
    wg = wb.reshape(-1, gs).astype(np.float32)
    s = np.abs(wg).sum(-1, keepdims=True) * (1.0 / gs)
    s = np.maximum(s.astype(bf16), bf16(1e-8)).astype(np.float32)
    d = (wg / s).astype(bf16).astype(np.float32)
    q = np.clip(np.round(d), -1., 1.)
    return (q.astype(bf16).astype(np.float32) * s).astype(bf16).reshape(w.shape)


def test_quant():
    rng = np.random.default_rng(2)
    for (R, C) in ((8192, 2048), (2048, 8192)):
        w = (rng.standard_normal((R, C)) / np.sqrt(C)).astype(np.float32)
        nc = build_quant_test(R, C)
        res = run_bass_kernel_spmd(nc, [dict(w=w)], core_ids=[0])
        got = res.results[0]["o"]
        ref = _ref_ternary(w)
        mism = int((got.astype(np.float32) != ref.astype(np.float32)).sum())
        print(f"QUANT {R}x{C}: mismatches {mism} / {got.size}")
        assert mism <= 50


def test_trmat():
    rng = np.random.default_rng(3)
    R, C = 8192, 2048
    nc = bass.Bass("TRN2", target_bir_lowering=False, debug=False)
    s_d = nc.dram_tensor("s", [R, C], BF16, kind="ExternalInput").ap()
    d_d = nc.dram_tensor("d", [C, R], BF16, kind="ExternalOutput").ap()
    with nc.Block() as blk:
        ctx = Ctx(nc, blk)
        alloc_common(ctx)
        ss = emit_transpose_mat(ctx, "tm", s_d, d_d, R, C)

        @blk.sync
        def _(sync):
            for s in ss:
                s.wait(sync)
    w = rng.standard_normal((R, C)).astype(bf16)
    res = run_bass_kernel_spmd(nc, [dict(s=w)], core_ids=[0])
    ok = np.array_equal(res.results[0]["d"].view(np.uint16), w.T.view(np.uint16))
    print(f"TRMAT: exact {ok}")
    assert ok


def test_ffn0():
    rng = np.random.default_rng(4)
    xnT = rng.standard_normal((D, T)).astype(bf16)
    H = 8192
    w1 = _ref_ternary((rng.standard_normal((H, D)) / np.sqrt(D)).astype(np.float32))
    w2 = _ref_ternary((rng.standard_normal((D, H)) / np.sqrt(H)).astype(np.float32))
    nc = bass.Bass("TRN2", target_bir_lowering=False, debug=False)
    xn_d = nc.dram_tensor("xnT", [D, T], BF16, kind="ExternalInput").ap()
    w1t_d = nc.dram_tensor("w1T", [D, H], BF16, kind="ExternalInput").ap()
    w2t_d = nc.dram_tensor("w2T", [H, D], BF16, kind="ExternalInput").ap()
    m0_d = nc.dram_tensor("m0", [D, T], BF16, kind="ExternalOutput").ap()
    with nc.Block() as blk:
        ctx = Ctx(nc, blk)
        alloc_common(ctx)
        ss = emit_ffn0(ctx, "f0", xn_d, w1t_d, w2t_d, m0_d)

        @blk.sync
        def _(sync):
            for s in ss:
                s.wait(sync)
    res = run_bass_kernel_spmd(nc, [dict(xnT=xnT, w1T=np.ascontiguousarray(w1.T),
                                         w2T=np.ascontiguousarray(w2.T))], core_ids=[0])
    got = res.results[0]["m0"].astype(np.float32)
    x = xnT.astype(np.float64).T
    h = x @ w1.astype(np.float64).T
    h = np.square(np.maximum(h, 0))
    ms = (h * h).mean(-1, keepdims=True)
    hn = h / np.sqrt(ms + float(np.finfo(np.float32).eps))
    ref = (hn @ w2.astype(np.float64).T).T.astype(np.float32)
    rel = np.sqrt(((got - ref) ** 2).mean() / (ref ** 2).mean())
    print(f"FFN0: relL2 {rel:.3e}")
    assert rel < 5e-3


def test_attn():
    rng = np.random.default_rng(5)
    TK = 4096
    q = (rng.standard_normal((T, D)) * 0.9).astype(bf16)
    k = (rng.standard_normal((TK, D)) * 0.9).astype(bf16)
    v3 = (rng.standard_normal((16, TK, 128)) * 0.9).astype(bf16)
    lq1 = (rng.standard_normal((1, 64)) * 0.1).astype(np.float32)
    lk1 = (rng.standard_normal((1, 64)) * 0.1).astype(np.float32)
    lq2 = (rng.standard_normal((1, 64)) * 0.1).astype(np.float32)
    lk2 = (rng.standard_normal((1, 64)) * 0.1).astype(np.float32)
    rms = (rng.standard_normal((1, 128)) * 0.5 + 1.0).astype(np.float32)
    wp = (rng.standard_normal((D, D)) * 0.02).astype(np.float32)

    nc = bass.Bass("TRN2", target_bir_lowering=False, debug=False)
    q_d = nc.dram_tensor("q", [T, D], BF16, kind="ExternalInput").ap()
    k_d = nc.dram_tensor("k", [TK, D], BF16, kind="ExternalInput").ap()
    v_d = nc.dram_tensor("v3", [16, TK, 128], BF16, kind="ExternalInput").ap()
    lq1_d = nc.dram_tensor("lq1", [1, 64], F32, kind="ExternalInput").ap()
    lk1_d = nc.dram_tensor("lk1", [1, 64], F32, kind="ExternalInput").ap()
    lq2_d = nc.dram_tensor("lq2", [1, 64], F32, kind="ExternalInput").ap()
    lk2_d = nc.dram_tensor("lk2", [1, 64], F32, kind="ExternalInput").ap()
    rms_d = nc.dram_tensor("rms", [1, 128], F32, kind="ExternalInput").ap()
    wp_d = nc.dram_tensor("wp", [D, D], F32, kind="ExternalInput").ap()
    a0_d = nc.dram_tensor("a0", [D, T], BF16, kind="ExternalOutput").ap()
    with nc.Block() as blk:
        ctx = Ctx(nc, blk)
        alloc_common(ctx)
        ss = emit_attn(ctx, "at", q_d, [(k_d, TK)], [(v_d, TK)],
                       lq1_d, lk1_d, lq2_d, lk2_d, rms_d, wp_d, a0_d)

        @blk.sync
        def _(sync):
            for s in ss:
                s.wait(sync)
    res = run_bass_kernel_spmd(nc, [dict(q=q, k=k, v3=v3, lq1=lq1, lk1=lk1, lq2=lq2,
                                         lk2=lk2, rms=rms, wp=wp)], core_ids=[0])
    got = res.results[0]["a0"].astype(np.float32)

    qf = q.astype(np.float32); kf = k.astype(np.float32)
    lam = float(np.exp((lq1 * lk1).sum()) - np.exp((lq2 * lk2).sum()) + LAMBDA_INIT)
    out = np.zeros((T, D), np.float32)
    for h in range(16):
        a_sets = []
        for s_i in range(2):
            qh = qf[:, (16 * s_i + h) * 64:(16 * s_i + h) * 64 + 64]
            kh = kf[:, (16 * s_i + h) * 64:(16 * s_i + h) * 64 + 64]
            sc = (qh @ kh.T) * 0.125
            sc = sc - sc.max(-1, keepdims=True)
            e = np.exp(sc)
            pr = e / e.sum(-1, keepdims=True)
            a_sets.append(pr @ v3[h].astype(np.float32))
        a = a_sets[0] - lam * a_sets[1]
        msq = (a * a).mean(-1, keepdims=True)
        nr = a / np.sqrt(msq + 1e-6)
        out[:, h * 128:(h + 1) * 128] = nr * rms[0] * (1.0 - LAMBDA_INIT)
    ref = (out @ wp).T.astype(np.float32)
    rel = np.sqrt(((got - ref) ** 2).mean() / (ref ** 2).mean())
    print(f"ATTN: relL2 {rel:.3e}")
    assert rel < 8e-3


def _ssd_dt_ref(dt_pre, dt_bias, A_log):
    dtv = dt_pre + dt_bias[:, None]
    dtv = np.log1p(np.exp(-np.abs(dtv))) + np.maximum(dtv, 0)
    A = -np.exp(A_log)
    return dtv, A[:, None] * dtv


def test_l1a():
    rng = np.random.default_rng(6)
    CD = 6144
    hbc = (rng.standard_normal((CD, T)) * 0.8).astype(bf16)
    tail = (rng.standard_normal((CD, 3)) * 0.8).astype(bf16)
    dt_pre = (rng.standard_normal((32, T)) * 0.5).astype(np.float32)
    convw = (rng.standard_normal((CD, 4)) * 0.5).astype(np.float32)
    convb = (rng.standard_normal((CD, 1)) * 0.02).astype(np.float32)
    dtb = (rng.standard_normal((32, 1)) * 0.1 + 1.0).astype(np.float32)
    alog = np.log(np.arange(1, 33, dtype=np.float32))[:, None]
    flag = np.ones((128, 1), np.float32)

    nc = bass.Bass("TRN2", target_bir_lowering=False, debug=False)
    hbc_d = nc.dram_tensor("hbc", [CD, T], BF16, kind="ExternalInput").ap()
    dtp_d = nc.dram_tensor("dtp", [32, T], F32, kind="ExternalInput").ap()
    tail_d = nc.dram_tensor("tail", [CD, 3], BF16, kind="ExternalInput").ap()
    flag_d = nc.dram_tensor("flag", [128, 1], F32, kind="ExternalInput").ap()
    cw_d = nc.dram_tensor("cw", [128, 192], F32, kind="ExternalInput").ap()
    cb_d = nc.dram_tensor("cb", [128, 48], F32, kind="ExternalInput").ap()
    dtb_d = nc.dram_tensor("dtb", [32, 1], F32, kind="ExternalInput").ap()
    al_d = nc.dram_tensor("al", [32, 1], F32, kind="ExternalInput").ap()
    st_d = nc.dram_tensor("stash", [CD, T], BF16, kind="ExternalOutput").ap()
    dtT_d = nc.dram_tensor("dtT", [32, T], F32, kind="ExternalOutput").ap()
    cst_d = nc.dram_tensor("cst", [8 * 128, 4096], F32, kind="ExternalOutput").ap()
    s8_d = nc.dram_tensor("s8", [128, 4096], F32, kind="ExternalOutput").ap()
    with nc.Block() as blk:
        ctx = Ctx(nc, blk)
        alloc_common(ctx)
        ss = emit_l1a(ctx, "la", hbc_d, dtp_d, tail_d, flag_d, cw_d, cb_d, dtb_d, al_d,
                      st_d, dtT_d, cst_d, s8_d)

        @blk.sync
        def _(sync):
            for s in ss:
                s.wait(sync)
    cw_l = np.ascontiguousarray(convw.reshape(48, 128, 4).transpose(1, 0, 2).reshape(128, 192))
    cb_l = np.ascontiguousarray(convb.reshape(48, 128).T)
    res = run_bass_kernel_spmd(nc, [dict(hbc=hbc, dtp=dt_pre, tail=tail, flag=flag,
                                         cw=cw_l, cb=cb_l, dtb=dtb, al=alog)], core_ids=[0])
    r = res.results[0]
    hb_ext = np.concatenate([tail.astype(np.float32), hbc.astype(np.float32)], 1)
    conv = sum(convw[:, k:k + 1] * hb_ext[:, k:k + T] for k in range(4)) + convb
    stash_ref = conv / (1.0 + np.exp(-conv))
    rel = np.sqrt(((r["stash"].astype(np.float32) - stash_ref) ** 2).mean() / (stash_ref ** 2).mean())
    print(f"L1A stash relL2 {rel:.3e}")
    dtv, abt = _ssd_dt_ref(dt_pre, dtb[:, 0], alog[:, 0])
    rel2 = np.abs(r["dtT"] - dtv).max()
    print(f"L1A dtT max err {rel2:.3e}")
    hs = stash_ref[:4096].reshape(32, 128, T)
    B = stash_ref[4096:5120].reshape(8, 128, T)
    S = np.zeros((128, 32, 128))
    okc = []
    for c in range(8):
        a = abt[:, c * CH:(c + 1) * CH]
        acum = np.cumsum(a, 1)
        dec = np.exp(acum[:, -1:] - acum)
        lam = np.exp(acum[:, -1])
        cst_ref = np.zeros((128, 32, 128))
        for h in range(32):
            X = hs[h][:, c * CH:(c + 1) * CH]
            Bd = B[h % 8][:, c * CH:(c + 1) * CH]
            w = dtv[h, c * CH:(c + 1) * CH] * dec[h]
            cst_ref[:, h, :] = (Bd * w[None, :]) @ X.T
            S[:, h, :] = lam[h] * S[:, h, :] + cst_ref[:, h, :]
        gc = r["cst"][c * 128:(c + 1) * 128].reshape(128, 32, 128)
        if c == 0:
            print("dev cst nan:", int(np.isnan(gc).sum()), "inf:", int(np.isinf(gc).sum()),
                  " ref nan:", int(np.isnan(cst_ref).sum()),
                  " dev head0 sample:", gc[0, 0, :3], " ref:", cst_ref[0, 0, :3])
            relh = [float(np.sqrt(np.nanmean((gc[:, h] - cst_ref[:, h]) ** 2) /
                                  max((cst_ref[:, h] ** 2).mean(), 1e-20))) for h in range(32)]
            print("per-head rel (c0):", ["%.0e" % x for x in relh])
        okc.append(np.sqrt(((gc - cst_ref) ** 2).mean() / (cst_ref ** 2).mean()))
    print(f"L1A cst relL2 per chunk: {['%.1e' % x for x in okc]}")
    s8g = r["s8"].reshape(128, 32, 128)
    rel3 = np.sqrt(((s8g - S) ** 2).mean() / (S ** 2).mean())
    print(f"L1A s8 relL2 {rel3:.3e}")
    assert rel < 8e-3 and max(okc) < 2e-2 and rel3 < 2e-2


def test_l1b():
    rng = np.random.default_rng(7)
    CD = 6144
    stash = (rng.standard_normal((CD, T)) * 0.7).astype(bf16)
    gate = (rng.standard_normal((4096, T)) * 0.8).astype(bf16)
    dtv = (np.abs(rng.standard_normal((32, T))) * 1.2 + 0.05).astype(np.float32)
    alog = np.log(np.arange(1, 33, dtype=np.float32))[:, None]
    s8in = (rng.standard_normal((128, 4096)) * 0.3).astype(np.float32)
    cst = (rng.standard_normal((8 * 128, 4096)) * 0.2).astype(np.float32)
    nw = (rng.standard_normal((128, 32)) * 0.3 + 1.0).astype(np.float32)
    dD = (rng.standard_normal((32, 1)) * 0.3 + 1.0).astype(np.float32)
    flag = np.ones((128, 1), np.float32)

    nc = bass.Bass("TRN2", target_bir_lowering=False, debug=False)
    st_d = nc.dram_tensor("stash", [CD, T], BF16, kind="ExternalInput").ap()
    dt_d = nc.dram_tensor("dtT", [32, T], F32, kind="ExternalInput").ap()
    gt_d = nc.dram_tensor("gate", [4096, T], BF16, kind="ExternalInput").ap()
    cst_d = nc.dram_tensor("cst", [8 * 128, 4096], F32, kind="ExternalInput").ap()
    s8_d = nc.dram_tensor("s8in", [128, 4096], F32, kind="ExternalInput").ap()
    fl_d = nc.dram_tensor("flag", [128, 1], F32, kind="ExternalInput").ap()
    nw_d = nc.dram_tensor("nw", [128, 32], F32, kind="ExternalInput").ap()
    dD_d = nc.dram_tensor("dD", [32, 1], F32, kind="ExternalInput").ap()
    al_d = nc.dram_tensor("al", [32, 1], F32, kind="ExternalInput").ap()
    n1_d = nc.dram_tensor("norm1", [4096, T], BF16, kind="ExternalOutput").ap()
    with nc.Block() as blk:
        ctx = Ctx(nc, blk)
        alloc_common(ctx)
        ss = emit_l1b(ctx, "lb", st_d, dt_d, gt_d, cst_d, s8_d, fl_d, nw_d, dD_d, al_d, n1_d)

        @blk.sync
        def _(sync):
            for s in ss:
                s.wait(sync)
    res = run_bass_kernel_spmd(nc, [dict(stash=stash, dtT=dtv, gate=gate, cst=cst,
                                         s8in=s8in, flag=flag, nw=nw, dD=dD, al=alog)], core_ids=[0])
    got = res.results[0]["norm1"].astype(np.float32)

    A = -np.exp(alog[:, 0])
    abt = A[:, None] * dtv
    hs = stash.astype(np.float32)[:4096].reshape(32, 128, T)
    Bm = stash.astype(np.float32)[4096:5120].reshape(8, 128, T)
    Cm = stash.astype(np.float32)[5120:6144].reshape(8, 128, T)
    S = s8in.reshape(128, 32, 128).copy()
    y_all = np.zeros((32, 128, T), np.float32)
    for c in range(8):
        a = abt[:, c * CH:(c + 1) * CH]
        acum = np.cumsum(a, 1)
        if c > 0:
            lamp = np.exp(np.cumsum(abt[:, (c - 1) * CH:c * CH], 1)[:, -1])
            for h in range(32):
                S[:, h, :] = lamp[h] * S[:, h, :] + cst[(c - 1) * 128:c * 128, h * 128:(h + 1) * 128]
        for h in range(32):
            g = h % 8
            X = hs[h][:, c * CH:(c + 1) * CH]
            Bd = Bm[g][:, c * CH:(c + 1) * CH]
            Cd = Cm[g][:, c * CH:(c + 1) * CH]
            Ls = np.exp(np.minimum(acum[h][None, :] - acum[h][:, None], 0)) * \
                 (np.arange(CH)[None, :] >= np.arange(CH)[:, None])
            G = Bd.T @ Cd
            M = G * Ls
            Xd = (X * dtv[h, c * CH:(c + 1) * CH][None, :])
            Yd = Xd @ M
            Yoff = S[:, h, :].T @ (Cd * np.exp(acum[h])[None, :])
            y_all[h][:, c * CH:(c + 1) * CH] = Yd + Yoff + dD[h, 0] * X
    gt32 = gate.astype(np.float32).reshape(32, 128, T)
    h32 = y_all * (gt32 / (1 + np.exp(-gt32)))
    ref = np.zeros((32, 128, T), np.float32)
    for g in range(8):
        blkv = h32[4 * g:4 * g + 4]
        msq = (blkv ** 2).mean(axis=(0, 1))
        rstd = 1.0 / np.sqrt(msq + 1e-6)
        for hh in range(4):
            h = 4 * g + hh
            ref[h] = h32[h] * rstd[None, :] * nw[:, h][:, None]
    refm = ref.reshape(4096, T)
    rel = np.sqrt(((got - refm) ** 2).mean() / (refm ** 2).mean())
    print(f"L1B: relL2 {rel:.3e}")
    assert rel < 2e-2


def test_moe():
    rng = np.random.default_rng(8)
    E, R, I2 = 8, 16, 1024
    xnT = (rng.standard_normal((D, T)) * 0.8).astype(bf16)
    gw = (rng.standard_normal((E, D)) / np.sqrt(D)).astype(np.float32)
    gla = (rng.standard_normal((R, D)) * 0.02).astype(np.float32)
    glb = (rng.standard_normal((E, R)) * 0.02).astype(np.float32)
    gup = (rng.standard_normal((E, D, 2 * I2)) / np.sqrt(D)).astype(np.float32)
    dwn = (rng.standard_normal((E, I2, D)) / np.sqrt(I2)).astype(np.float32)
    gupA = (rng.standard_normal((E, R, D)) * 0.02).astype(np.float32)
    gupB = (rng.standard_normal((E, 2 * I2, R)) * 0.02).astype(np.float32)
    dwnA = (rng.standard_normal((E, R, I2)) * 0.02).astype(np.float32)
    dwnB = (rng.standard_normal((E, D, R)) * 0.02).astype(np.float32)

    nc = bass.Bass("TRN2", target_bir_lowering=False, debug=False)
    xn_d = nc.dram_tensor("xnT", [D, T], BF16, kind="ExternalInput").ap()
    xn32_d = nc.dram_tensor("xn32", [D, T], F32, kind="ExternalInput").ap()
    gwT_d = nc.dram_tensor("gwT", [D, E], F32, kind="ExternalInput").ap()
    glaT_d = nc.dram_tensor("glaT", [D, R], F32, kind="ExternalInput").ap()
    glbT_d = nc.dram_tensor("glbT", [R, E], F32, kind="ExternalInput").ap()
    gAT_d = nc.dram_tensor("gAT", [D, 128], F32, kind="ExternalInput").ap()
    gBT_d = nc.dram_tensor("gBT", [R, E * D], F32, kind="ExternalInput").ap()
    dAT_d = nc.dram_tensor("dAT", [I2, 128], F32, kind="ExternalInput").ap()
    dBT_d = nc.dram_tensor("dBT", [R, E * D], F32, kind="ExternalInput").ap()
    gup_d = nc.dram_tensor("gup_bf", [E * D, 2 * I2], BF16, kind="ExternalInput").ap()
    dwn_d = nc.dram_tensor("dwn_bf", [E * I2, D], BF16, kind="ExternalInput").ap()
    f1_d = nc.dram_tensor("f1", [D, T], BF16, kind="ExternalOutput").ap()
    with nc.Block() as blk:
        ctx = Ctx(nc, blk)
        alloc_common(ctx)
        ss = emit_moe(ctx, "me", xn_d, xn32_d, gwT_d, glaT_d, glbT_d, gAT_d, gBT_d, dAT_d, dBT_d,
                      gup_d, dwn_d, f1_d)

        @blk.sync
        def _(sync):
            for s in ss:
                s.wait(sync)
    ins = dict(
        xnT=xnT, xn32=xnT.astype(np.float32), gwT=np.ascontiguousarray(gw.T), glaT=np.ascontiguousarray(gla.T),
        glbT=np.ascontiguousarray(glb.T),
        gAT=np.ascontiguousarray(gupA.transpose(2, 0, 1).reshape(D, 128)),
        gBT=np.ascontiguousarray(gupB.transpose(2, 0, 1).reshape(R, E * D)),
        dAT=np.ascontiguousarray(dwnA.transpose(2, 0, 1).reshape(I2, 128)),
        dBT=np.ascontiguousarray(dwnB.transpose(2, 0, 1).reshape(R, E * D)),
        gup_bf=np.ascontiguousarray(gup.reshape(E * D, 2 * I2)).astype(bf16),
        dwn_bf=np.ascontiguousarray(dwn.reshape(E * I2, D)).astype(bf16))
    res = run_bass_kernel_spmd(nc, [ins], core_ids=[0])
    got = res.results[0]["f1"].astype(np.float32)

    xf = xnT.astype(np.float32).T
    gate_delta = 2.0 * (glb @ gla)
    logits = xf @ gw.T + xf @ gate_delta.T
    m = logits.max(-1, keepdims=True)
    p = np.exp(logits - m); p /= p.sum(-1, keepdims=True)
    srt_ = np.sort(p, -1)
    m1, m2 = srt_[:, -1], srt_[:, -2]
    maskt = (p == m1[:, None]) | (p == m2[:, None])
    c = p * maskt / (m1 + m2)[:, None]
    y = np.zeros((T, D), np.float32)
    for e in range(E):
        gu = xf @ gup[e] + 2.0 * (xf @ gupA[e].T) @ gupB[e].T
        gts, up = gu[:, :I2], gu[:, I2:]
        h = (gts / (1 + np.exp(-gts))) * up
        hc = h * c[:, e:e + 1]
        y += hc @ dwn[e] + 2.0 * (hc @ dwnA[e].T) @ dwnB[e].T
    ref = y.T
    rel = np.sqrt(((got - ref) ** 2).mean() / (ref ** 2).mean())
    print(f"MOE: relL2 {rel:.3e}")
    assert rel < 2e-2


def test_glue():
    rng = np.random.default_rng(9)

    xnT = (rng.standard_normal((D, T)) * 0.8).astype(bf16)
    wT = (rng.standard_normal((D, 10272)) * 0.02).astype(bf16)
    nc = bass.Bass("TRN2", target_bir_lowering=False, debug=False)
    xn_d = nc.dram_tensor("xnT", [D, T], BF16, kind="ExternalInput").ap()
    w_d = nc.dram_tensor("wT", [D, 10272], BF16, kind="ExternalInput").ap()
    g_d = nc.dram_tensor("gate", [4096, T], BF16, kind="ExternalOutput").ap()
    h_d = nc.dram_tensor("hbc", [6144, T], BF16, kind="ExternalOutput").ap()
    dt_d = nc.dram_tensor("dtp", [32, T], F32, kind="ExternalOutput").ap()
    with nc.Block() as blk:
        ctx = Ctx(nc, blk)
        alloc_common(ctx)
        ss = emit_proj_g(ctx, "ip", xn_d, w_d,
                         [(g_d, 0, 4096), (h_d, 0, 6144), (dt_d, 0, 32, F32)], D)

        @blk.sync
        def _(sync):
            for s in ss:
                s.wait(sync)
    res = run_bass_kernel_spmd(nc, [dict(xnT=xnT, wT=wT)], core_ids=[0])
    r = res.results[0]
    full = wT.astype(np.float32).T @ xnT.astype(np.float32)
    for nm, got, lo, hi in (("gate", r["gate"], 0, 4096), ("hbc", r["hbc"], 4096, 10240),
                            ("dtp", r["dtp"], 10240, 10272)):
        ref = full[lo:hi]
        rel = np.sqrt(((got.astype(np.float32) - ref) ** 2).mean() / (ref ** 2).mean())
        print(f"GLUE inproj {nm}: relL2 {rel:.3e}")
        assert rel < 6e-3

    n1 = (rng.standard_normal((4096, T)) * 0.8).astype(bf16)
    wT2 = (rng.standard_normal((4096, D)) * 0.02).astype(bf16)
    nc = bass.Bass("TRN2", target_bir_lowering=False, debug=False)
    n1_d = nc.dram_tensor("n1", [4096, T], BF16, kind="ExternalInput").ap()
    w2_d = nc.dram_tensor("wT2", [4096, D], BF16, kind="ExternalInput").ap()
    a1_d = nc.dram_tensor("a1", [D, T], BF16, kind="ExternalOutput").ap()
    with nc.Block() as blk:
        ctx = Ctx(nc, blk)
        alloc_common(ctx)
        ss = emit_proj_g(ctx, "op", n1_d, w2_d, [(a1_d, 0, D)], 4096)

        @blk.sync
        def _(sync):
            for s in ss:
                s.wait(sync)
    res = run_bass_kernel_spmd(nc, [dict(n1=n1, wT2=wT2)], core_ids=[0])
    ref = wT2.astype(np.float32).T @ n1.astype(np.float32)
    got = res.results[0]["a1"].astype(np.float32)
    rel = np.sqrt(((got - ref) ** 2).mean() / (ref ** 2).mean())
    print(f"GLUE outproj: relL2 {rel:.3e}")
    assert rel < 6e-3

    a = (rng.standard_normal((D, T))).astype(bf16)
    b = (rng.standard_normal((D, T))).astype(bf16)
    nc = bass.Bass("TRN2", target_bir_lowering=False, debug=False)
    a_d = nc.dram_tensor("a", [D, T], BF16, kind="ExternalInput").ap()
    b_d = nc.dram_tensor("b", [D, T], BF16, kind="ExternalInput").ap()
    o_d = nc.dram_tensor("o", [D, T], BF16, kind="ExternalOutput").ap()
    with nc.Block() as blk:
        ctx = Ctx(nc, blk)
        alloc_common(ctx)
        ss = emit_addfm(ctx, "ad", a_d, b_d, o_d)

        @blk.sync
        def _(sync):
            for s in ss:
                s.wait(sync)
    res = run_bass_kernel_spmd(nc, [dict(a=a, b=b)], core_ids=[0])
    ref = (a.astype(np.float32) + b.astype(np.float32)).astype(bf16)
    ok = np.array_equal(res.results[0]["o"].view(np.uint16), ref.view(np.uint16))
    print(f"GLUE addfm: exact {ok}")
    assert ok

    x = rng.standard_normal((T, D)).astype(np.float32)
    ms = [(rng.standard_normal((D, T))).astype(bf16) for _ in range(3)]
    nc = bass.Bass("TRN2", target_bir_lowering=False, debug=False)
    x_d = nc.dram_tensor("x", [T, D], F32, kind="ExternalInput").ap()
    m_ds = [nc.dram_tensor(f"m{i}", [D, T], BF16, kind="ExternalInput").ap() for i in range(3)]
    o_d = nc.dram_tensor("o", [T, D], F32, kind="ExternalOutput").ap()
    with nc.Block() as blk:
        ctx = Ctx(nc, blk)
        alloc_common(ctx)
        ss = emit_final(ctx, "fi", x_d, m_ds, o_d)

        @blk.sync
        def _(sync):
            for s in ss:
                s.wait(sync)
    res = run_bass_kernel_spmd(nc, [dict(x=x, m0=ms[0], m1=ms[1], m2=ms[2])], core_ids=[0])
    ref = x + sum(m.astype(np.float32).T for m in ms)
    err = np.abs(res.results[0]["o"] - ref).max()
    print(f"GLUE final: max abs err {err:.3e}")
    assert err < 1e-5

    x2 = rng.standard_normal((T, D)).astype(np.float32)
    wcb = rng.standard_normal((1024, 512)).astype(np.float32)
    nc = bass.Bass("TRN2", target_bir_lowering=False, debug=False)
    x_d = nc.dram_tensor("x", [T, D], F32, kind="ExternalInput").ap()
    w_d = nc.dram_tensor("w", [1024, 512], F32, kind="ExternalInput").ap()
    xn_d = nc.dram_tensor("xnT", [D, T], BF16, kind="ExternalOutput").ap()
    x32_d = nc.dram_tensor("xn32", [T, D], F32, kind="ExternalOutput").ap()
    wb_d = nc.dram_tensor("wb", [1024, 512], BF16, kind="ExternalOutput").ap()
    with nc.Block() as blk:
        ctx = Ctx(nc, blk)
        alloc_common(ctx)
        ss = emit_ln(ctx, "ln", x_d, xn_d, out32_dram=x32_d)
        ss2 = emit_castbounce(ctx, "cb", w_d, wb_d, 1024, 512)

        @blk.sync
        def _(sync):
            for s in ss + ss2:
                s.wait(sync)
    res = run_bass_kernel_spmd(nc, [dict(x=x2, w=wcb)], core_ids=[0])
    xm = x2 - x2.mean(-1, keepdims=True)
    ref = xm / np.sqrt((xm ** 2).mean(-1, keepdims=True) + 1e-5)
    got32 = res.results[0]["xn32"]
    rel = np.sqrt(((got32 - ref) ** 2).mean() / (ref ** 2).mean())
    okcb = np.array_equal(res.results[0]["wb"].view(np.uint16), wcb.astype(bf16).view(np.uint16))
    print(f"GLUE ln32: relL2 {rel:.3e}  castbounce exact {okcb}")
    assert rel < 1e-4 and okcb


if __name__ == "__main__":
    import sys
    which = sys.argv[1] if len(sys.argv) > 1 else "all"
    if which in ("ln", "all"):
        test_ln()
    if which in ("qkv", "all"):
        test_qkv()
    if which in ("quant", "all"):
        test_quant()
    if which in ("trmat", "all"):
        test_trmat()
    if which in ("ffn0", "all"):
        test_ffn0()
    if which in ("attn", "all"):
        test_attn()
    if which in ("l1a", "all"):
        test_l1a()
    if which in ("l1b", "all"):
        test_l1b()
    if which in ("moe", "all"):
        test_moe()
    if which in ("glue", "all"):
        test_glue()
